# version 11
# baseline (speedup 1.0000x reference)
import numpy as np
from contextlib import ExitStack

import concourse.bass as bass
import concourse.mybir as mybir
from concourse.bass_utils import run_bass_kernel_spmd

F32 = mybir.dt.float32
BF16 = mybir.dt.bfloat16
I32 = mybir.dt.int32
AF = mybir.ActivationFunctionType
ALU = mybir.AluOpType
AX = mybir.AxisListType

B = 2
S = 16384
D = 1024
NCORE = 8
GPB = 4
TPC = S // GPB
NQT = TPC // 128
NGRP = NQT // 4
NEG = -1e30
LN_EPS = 1e-5
CEXP = 384
NTE = CEXP // 128


def own_token_idx(g):
    idx = []
    for i in range(NQT):
        qt = 4 * i + g
        idx.append(np.arange(qt * 128, qt * 128 + 128))
    return np.concatenate(idx)


def make_dmask(g):
    ql = np.arange(128)[:, None]
    j = np.arange(512)[None, :]
    return np.where(j <= 128 * g + ql, 0.0, NEG).astype(np.float32)


class Sem:

    def __init__(self, stack, nc, name):
        self.h = stack.enter_context(nc.semaphore(name))
        self.n = 0

    def inc(self, inst, k=1):
        inst.then_inc(self.h, k)
        self.n += k
        return self.n

    def dma(self, inst):
        return self.inc(inst, 16)

    def wait(self, eng, n=None):
        eng.wait_ge(self.h, self.n if n is None else n)


class Env:

    def __init__(self, nc, stack):
        self.nc = nc
        self.block = stack.enter_context(nc.Block())
        self.ld = Sem(stack, nc, "ld")
        self.ls = Sem(stack, nc, "ls")
        self.pe = Sem(stack, nc, "pe")
        self.ve = Sem(stack, nc, "ve")
        self.se = Sem(stack, nc, "se")
        self.so = Sem(stack, nc, "so")
        self.cc = Sem(stack, nc, "cc")
        self.cg = Sem(stack, nc, "cg")
        self.pS = stack.enter_context(nc.psum_tensor("pS", [128, 2048], F32))
        self.pPV = stack.enter_context(nc.psum_tensor("pPV", [128, 1024], F32))
        self.pT0 = stack.enter_context(nc.psum_tensor("pT0", [128, 512], BF16))


def emit_barrier(E):
    block = E.block
    sems = [E.ld, E.ls, E.pe, E.ve, E.se, E.so]
    for sec in (block.tensor, block.vector, block.scalar, block.sync,
                block.gpsimd):
        @sec
        def _(e):
            for s in sems:
                s.wait(e)




def emit_flash_group(E, a, P, st8):
    nc, block = E.nc, E.block
    ld, ls, pe, ve, se, so = E.ld, E.ls, E.pe, E.ve, E.se, E.so
    pS, pPV = E.pS, E.pPV
    qT_sb, kTm_sb, vm_sb = P["qT_sb"], P["kTm_sb"], P["vm_sb"]
    p_sb, pT_sb, O_sb, stat = P["p_sb"], P["pT_sb"], P["O_sb"], P["stat"]
    dm_sb, eye_bf, pTt = P["dm_sb"], P["eye_bf"], P["pTt"]
    kT_buf, v_buf = P["kT_buf"], P["v_buf"]
    SC = P["SC"]
    causal = P["causal"]
    nmac = P["nmac"]

    def n512(t):
        return (4 * a + t + 1) if causal else 32

    def nch_of(t, kb):
        return max(0, min(2, n512(t) - 2 * kb))

    def emit_kv_load(kb):
        slot = kb % 2
        guard = st8["kvfree"][slot]

        @block.sync
        def _(sy):
            sy.wait_ge(pe.h, guard)
            ls.dma(sy.dma_start(
                out=kTm_sb[:, slot * 8192:(slot + 1) * 8192]
                    .rearrange("p (c t) -> p c t", c=8),
                in_=kT_buf[:, kb * 1024:(kb + 1) * 1024]
                    .rearrange("(c p) t -> p c t", p=128)))
            ls.dma(sy.dma_start(
                out=vm_sb[:, slot * 8192:(slot + 1) * 8192]
                    .rearrange("p (a d) -> p a d", a=8),
                in_=v_buf[kb * 1024:(kb + 1) * 1024, :]
                    .rearrange("(a p) d -> p a d", p=128)))
        st8["kvready"][slot] = ls.n

    emit_kv_load(0)
    for kb in range(nmac):
        kslot = kb % 2
        kv_ready = st8["kvready"][kslot]
        kbase = kslot * 8192
        if kb + 1 < nmac:
            emit_kv_load(kb + 1)
        first = (kb == 0)
        seq = [t for t in range(4) if nch_of(t, kb) > 0]

        def emit_S(t):
            nch = nch_of(t, kb)
            sl = t % 2
            sbase = sl * 1024
            w_exp = st8["exp"][sl]
            w_sts = st8["sts"][sl]

            @block.tensor
            def _(te):
                te.wait_ge(ls.h, kv_ready)
                te.wait_ge(se.h, w_exp)
                te.wait_ge(ve.h, w_sts)
                for ch in range(nch):
                    for c in range(8):
                        inst = te.matmul(
                            pS[:, sbase + ch * 512: sbase + (ch + 1) * 512],
                            qT_sb[:, c * 512 + t * 128: c * 512 + t * 128 + 128],
                            kTm_sb[:, kbase + c * 1024 + ch * 512:
                                   kbase + c * 1024 + (ch + 1) * 512],
                            start=(c == 0), stop=(c == 7))
                pe.inc(inst)
            return pe.n

        def emit_softmax(t, s_done):
            nch = nch_of(t, kb)
            sl = t % 2
            sbase = sl * 1024
            ncol = nch * 512

            @block.vector
            def _(v):
                v.wait_ge(pe.h, s_done)
                ve.wait(v, st8["oup"])
                if causal and (4 * a + t) // 2 == kb:
                    lc = (4 * a + t) % 2
                    ve.inc(v.tensor_tensor(
                        pS[:, sbase + lc * 512: sbase + (lc + 1) * 512],
                        pS[:, sbase + lc * 512: sbase + (lc + 1) * 512],
                        dm_sb[:, :], ALU.add))
                    ve.wait(v)
                ve.inc(v.reduce_max(stat[:, 8 + t:9 + t],
                                    pS[:, sbase: sbase + ncol], axis=AX.X))
                ve.wait(v)
                if first:
                    ve.inc(v.tensor_copy(stat[:, t:t + 1], stat[:, 8 + t:9 + t]))
                else:
                    ve.inc(v.tensor_max(stat[:, 12 + t:13 + t], stat[:, t:t + 1],
                                        stat[:, 8 + t:9 + t]))
                    ve.wait(v)
                    ve.inc(v.tensor_sub(stat[:, 16 + t:17 + t], stat[:, t:t + 1],
                                        stat[:, 12 + t:13 + t]))
                    ve.wait(v)
                    ve.inc(v.tensor_copy(stat[:, t:t + 1], stat[:, 12 + t:13 + t]))
                ve.wait(v)
                ve.inc(v.tensor_scalar_mul(stat[:, 20 + t:21 + t],
                                           stat[:, t:t + 1], -SC))
            st8["sts"][sl] = ve.n

            @block.scalar
            def _(s):
                ve.wait(s)
                s.wait_ge(pe.h, st8["ptdone"][sl])
                if not first:
                    se.inc(s.activation(stat[:, 16 + t:17 + t],
                                        stat[:, 16 + t:17 + t],
                                        AF.Exp, bias=0.0, scale=SC))
                se.inc(s.activation(p_sb[:, sbase: sbase + ncol],
                                    pS[:, sbase: sbase + ncol], AF.Exp,
                                    bias=stat[:, 20 + t:21 + t], scale=SC,
                                    accum_out=stat[:, 8 + t:9 + t]))
            st8["exp"][sl] = se.n

            @block.vector
            def _(v):
                se.wait(v)
                if first:
                    ve.inc(v.tensor_copy(stat[:, 4 + t:5 + t], stat[:, 8 + t:9 + t]))
                else:
                    ve.inc(v.scalar_tensor_tensor(stat[:, 4 + t:5 + t],
                                                  stat[:, 4 + t:5 + t],
                                                  stat[:, 16 + t:17 + t],
                                                  stat[:, 8 + t:9 + t],
                                                  ALU.mult, ALU.add))
            st8["oup"] = ve.n

        def emit_ptpv(t):
            nch = nch_of(t, kb)
            sl = t % 2
            sbase = sl * 1024
            ve_ptb = ve.n
            pe_ptb = pe.n

            @block.tensor
            def _(te):
                te.wait_ge(se.h, st8["exp"][sl])
                te.wait_ge(ve.h, ve_ptb)
                for ch in range(nch):
                    par = pTt[ch % 2]
                    for sf in range(4):
                        inst = te.transpose(
                            par[:, sf * 128:(sf + 1) * 128],
                            p_sb[:, sbase + ch * 512 + sf * 128:
                                 sbase + ch * 512 + (sf + 1) * 128],
                            eye_bf[:, :])
                    pe.inc(inst)
            st8["ptdone"][sl] = pe.n

            @block.vector
            def _(v):
                v.wait_ge(pe.h, st8["pvdone"][sl])
                for ch in range(nch):
                    v.wait_ge(pe.h, pe_ptb + ch + 1)
                    ve.inc(v.tensor_copy(
                        pT_sb[:, sbase + ch * 512: sbase + (ch + 1) * 512],
                        pTt[ch % 2][:, :]))
            cp_done = ve.n

            @block.tensor
            def _(te):
                te.wait_ge(ve.h, cp_done)
                te.wait_ge(ve.h, st8["odone"])
                nk = nch * 4
                for ks in range(nk):
                    for h in range(2):
                        inst = te.matmul(
                            pPV[:, h * 512:(h + 1) * 512],
                            pT_sb[:, sbase + ks * 128: sbase + (ks + 1) * 128],
                            vm_sb[:, kbase + ks * 1024 + h * 512:
                                  kbase + ks * 1024 + (h + 1) * 512],
                            start=(ks == 0), stop=(ks == nk - 1))
                pe.inc(inst)
            st8["pvdone"][sl] = pe.n
            pv_done = pe.n

            @block.vector
            def _(v):
                v.wait_ge(pe.h, pv_done)
                if first:
                    ve.inc(v.tensor_copy(O_sb[:, t * 1024:(t + 1) * 1024],
                                         pPV[:, :]))
                else:
                    ve.inc(v.scalar_tensor_tensor(
                        O_sb[:, t * 1024:(t + 1) * 1024],
                        O_sb[:, t * 1024:(t + 1) * 1024],
                        stat[:, 16 + t:17 + t], pPV[:, :], ALU.mult, ALU.add))
            st8["odone"] = ve.n

        s_done = {}
        s_done[seq[0]] = emit_S(seq[0])
        for idx, t in enumerate(seq):
            emit_softmax(t, s_done[t])
            if idx + 1 < len(seq):
                s_done[seq[idx + 1]] = emit_S(seq[idx + 1])
            emit_ptpv(t)
        st8["kvfree"][kslot] = pe.n




def precast_plan(io):
    L = []
    for e in [8] + list(range(8)):
        if e == 8:
            L += [(io["wbg"][8], io["sgT"], None), (io["wbu"][8], io["suT"], None),
                  (io["wbd"][8], io["sdT"], 8)]
        else:
            L += [(io["wbg"][e], io["wgT"][e], None),
                  (io["wbu"][e], io["wuT"][e], None),
                  (io["wbd"][e], io["wdT"][e], e)]
    L += [(io["w1bf"], io["W1T"], None), (io["w2bf"], io["W2T"], None),
          (io["w3bf"], io["W3T"], "mlp")]
    return L


def emit_precast_chunk(E, plan, lo, hi):
    block, ld, pe = E.block, E.ld, E.pe
    chunk = plan[lo:hi]
    if not chunk:
        return
    gate = pe.n

    @block.gpsimd
    def _(g):
        g.wait_ge(pe.h, gate)
        for dst, srcap, key in chunk:
            ld.dma(g.dma_start(out=dst, in_=srcap))
            if key == "mlp":
                E.mlp_cast = ld.n
            elif key is not None:
                E.cast_done[key] = ld.n



def emit_l0_attn(E, io):
    nc, block = E.nc, E.block
    ld, ls, pe, ve, se, so = E.ld, E.ls, E.pe, E.ve, E.se, E.so
    pS, pPV, pT0 = E.pS, E.pPV, E.pT0
    xb, xq = io["xb"], io["xq0"]
    xb_bf, xq0bf, x1bf = io["xb_bf"], io["xq0bf"], io["x1bf"]
    wqT, wkT, wvT, woT = io["wqT"], io["wkT"], io["wvT"], io["woT"]
    eye, dmask = io["eye"], io["dmask"]
    x1q = io["x1q_buf"]
    kT_buf, v_buf, qT_buf = io["kT_buf"], io["v_buf"], io["qT_buf"]

    stB = ExitStack()
    pT1 = stB.enter_context(nc.psum_tensor("b1_pT1", [128, 512], BF16))
    pTt = [pT0, pT1]
    wo_sb = stB.enter_context(nc.sbuf_tensor("b1_wo", [128, 8 * 1024], BF16))
    eye_bf = stB.enter_context(nc.sbuf_tensor("b1_eye", [128, 128], BF16))
    dm_sb = stB.enter_context(nc.sbuf_tensor("b1_dm", [128, 512], F32))
    stat = stB.enter_context(nc.sbuf_tensor("b1_stat", [128, 24], F32))
    ln_sb = stB.enter_context(nc.sbuf_tensor("b1_ln", [128, 16], F32))

    stA = ExitStack()
    wq_sb = stA.enter_context(nc.sbuf_tensor("b1_wq", [128, 8 * 1024], BF16))
    wk_sb = stA.enter_context(nc.sbuf_tensor("b1_wk", [128, 8 * 1024], BF16))
    wv_sb = stA.enter_context(nc.sbuf_tensor("b1_wv", [128, 8 * 1024], BF16))
    xT = stA.enter_context(nc.sbuf_tensor("b1_xT", [128, 2 * 8 * 512], BF16))
    kv_stage = stA.enter_context(nc.sbuf_tensor("b1_kvst", [128, 8 * 512], BF16))
    v_stage = stA.enter_context(nc.sbuf_tensor("b1_vst", [128, 4 * 1024], BF16))

    cast_marks = []

    @block.gpsimd
    def _(g):
        for w_sb, src in ((wq_sb, wqT), (wk_sb, wkT), (wv_sb, wvT), (wo_sb, woT)):
            for c in range(8):
                ld.dma(g.dma_start(out=w_sb[:, c * 1024:(c + 1) * 1024],
                                   in_=src[c * 128:(c + 1) * 128, :]))
        ld.dma(g.dma_start(out=eye_bf[:, :], in_=eye))
        for ch in range(8):
            ld.dma(g.dma_start(out=xb_bf[ch * 2048:(ch + 1) * 2048, :],
                               in_=xb[ch * 2048:(ch + 1) * 2048, :]))
            cast_marks.append(ld.n)
        for ch in range(2):
            ld.dma(g.dma_start(out=xq0bf[ch * 2048:(ch + 1) * 2048, :],
                               in_=xq[ch * 2048:(ch + 1) * 2048, :]))
            cast_marks.append(ld.n)

    @block.sync
    def _(sy):
        ls.dma(sy.dma_start(out=dm_sb[:, :], in_=dmask))

    xt_used = [pe.n, pe.n]

    def emit_group_xT(src_bf, grp, cast_mark=0):
        slot = grp % 2
        base = slot * 4096
        guard = xt_used[slot]

        @block.sync
        def _(sy):
            sy.wait_ge(pe.h, guard)
            if cast_mark:
                sy.wait_ge(ld.h, cast_mark)
            for c in range(8):
                ls.dma(sy.dma_start(
                    out=xT[:, base + c * 512: base + (c + 1) * 512],
                    in_=src_bf[grp * 512:(grp + 1) * 512, c * 128:(c + 1) * 128],
                    transpose=True))
        return base, ls.n

    def emit_proj_featT(w_sb, dst_stage, xbase, xready):
        se_base = se.n
        pe_base = pe.n

        @block.tensor
        def _(te):
            te.wait_ge(ls.h, xready)
            te.wait_ge(se.h, se_base)
            for oc in range(8):
                if oc >= 4:
                    te.wait_ge(se.h, se_base + oc - 3)
                for c in range(8):
                    inst = te.matmul(
                        pS[:, (oc % 4) * 512:(oc % 4 + 1) * 512],
                        w_sb[:, c * 1024 + oc * 128: c * 1024 + oc * 128 + 128],
                        xT[:, xbase + c * 512: xbase + (c + 1) * 512],
                        start=(c == 0), stop=(c == 7))
                pe.inc(inst)

        @block.scalar
        def _(s):
            for oc in range(8):
                s.wait_ge(pe.h, pe_base + oc + 1)
                se.inc(s.activation(dst_stage[:, oc * 512:(oc + 1) * 512],
                                    pS[:, (oc % 4) * 512:(oc % 4 + 1) * 512],
                                    AF.Copy))

    nfo = emit_group_xT(xb_bf, 0, cast_marks[0])
    for grp in range(32):
        xbase, xready = nfo
        if grp + 1 < 32:
            nfo = emit_group_xT(xb_bf, grp + 1, cast_marks[(grp + 1) // 4])
        else:
            nfo = emit_group_xT(xq0bf, 0, cast_marks[8])
        emit_proj_featT(wk_sb, kv_stage, xbase, xready)
        se_k = se.n

        @block.sync
        def _(sy):
            sy.wait_ge(se.h, se_k)
            for oc in range(8):
                so.dma(sy.dma_start(
                    out=kT_buf[oc * 128:(oc + 1) * 128, grp * 512:(grp + 1) * 512],
                    in_=kv_stage[:, oc * 512:(oc + 1) * 512]))

        ve_vbase = ve.n
        pe_vbase = pe.n
        for t in range(4):
            @block.tensor
            def _(te):
                if t == 0:
                    te.wait_ge(ve.h, ve_vbase)
                else:
                    te.wait_ge(ve.h, ve_vbase + t)
                for c in range(8):
                    for h in range(2):
                        inst = te.matmul(
                            pPV[:, h * 512:(h + 1) * 512],
                            xT[:, xbase + c * 512 + t * 128:
                               xbase + c * 512 + t * 128 + 128],
                            wv_sb[:, c * 1024 + h * 512: c * 1024 + (h + 1) * 512],
                            start=(c == 0), stop=(c == 7))
                pe.inc(inst)

            @block.vector
            def _(v):
                v.wait_ge(pe.h, pe_vbase + t + 1)
                ve.inc(v.tensor_copy(v_stage[:, t * 1024:(t + 1) * 1024], pPV[:, :]))

        xt_used[grp % 2] = pe.n
        ve_v = ve.n

        @block.sync
        def _(sy):
            sy.wait_ge(ve.h, ve_v)
            for t in range(4):
                so.dma(sy.dma_start(
                    out=v_buf[grp * 512 + t * 128: grp * 512 + (t + 1) * 128, :],
                    in_=v_stage[:, t * 1024:(t + 1) * 1024]))

        @block.tensor
        def _(te):
            so.wait(te)

    for grp in range(8):
        xbase, xready = nfo
        if grp + 1 < 8:
            nfo = emit_group_xT(xq0bf, grp + 1, cast_marks[8 + (grp + 1) // 4])
        emit_proj_featT(wq_sb, kv_stage, xbase, xready)
        xt_used[grp % 2] = pe.n
        se_q = se.n

        @block.sync
        def _(sy):
            sy.wait_ge(se.h, se_q)
            for oc in range(8):
                so.dma(sy.dma_start(
                    out=qT_buf[oc * 128:(oc + 1) * 128, grp * 512:(grp + 1) * 512],
                    in_=kv_stage[:, oc * 512:(oc + 1) * 512]))

        @block.tensor
        def _(te):
            so.wait(te)

    cast_plan = precast_plan(io)
    E.cast_done = {}
    E.mlp_cast = None

    stA.close()
    qT_sb = stB.enter_context(nc.sbuf_tensor("b1_qT", [128, 8 * 512], BF16))
    kTm_sb = stB.enter_context(nc.sbuf_tensor("b1_kTm", [128, 8 * 2048], BF16))
    vm_sb = stB.enter_context(nc.sbuf_tensor("b1_vm", [128, 16 * 1024], BF16))
    p_sb = stB.enter_context(nc.sbuf_tensor("b1_p", [128, 2048], BF16))
    pT_sb = stB.enter_context(nc.sbuf_tensor("b1_pT", [128, 16 * 128], BF16))
    O_sb = stB.enter_context(nc.sbuf_tensor("b1_O", [128, 4 * 1024], F32))
    Obf_sb = stB.enter_context(nc.sbuf_tensor("b1_Obf", [128, 4 * 1024], BF16))
    oT_sb = stB.enter_context(nc.sbuf_tensor("b1_oT", [128, 8 * 512], BF16))
    xf_sb = stB.enter_context(nc.sbuf_tensor("b1_xf", [128, 4 * 1024], F32))
    res_sb = stB.enter_context(nc.sbuf_tensor("b1_res", [128, 4 * 1024], F32))
    sq_sb = stB.enter_context(nc.sbuf_tensor("b1_sq", [128, 1024], F32))
    xo_sb = stB.enter_context(nc.sbuf_tensor("b1_xo", [128, 4 * 1024], F32))

    st8 = dict(exp=[se.n, se.n], sts=[ve.n, ve.n], ptdone=[pe.n, pe.n],
               pvdone=[pe.n, pe.n], oup=ve.n, odone=ve.n, kvfree=[pe.n, pe.n],
               kvready=[ls.n, ls.n])
    for a in range(NGRP):
        @block.sync
        def _(sy):
            pe.wait(sy)
            so.wait(sy)
            for c in range(8):
                ls.dma(sy.dma_start(
                    out=qT_sb[:, c * 512:(c + 1) * 512],
                    in_=qT_buf[c * 128:(c + 1) * 128, a * 512:(a + 1) * 512]))

        @block.sync
        def _(sy):
            ve.wait(sy)
            for t in range(4):
                ls.dma(sy.dma_start(
                    out=xf_sb[:, t * 1024:(t + 1) * 1024],
                    in_=xq[a * 512 + t * 128: a * 512 + (t + 1) * 128, :]))

        if a >= 1:
            emit_precast_chunk(E, cast_plan, (a - 1) * 5, a * 5)
        if a == NGRP - 1:
            emit_precast_chunk(E, cast_plan, (NGRP - 2) * 5, len(cast_plan))

        emit_flash_group(E, a, dict(
            causal=True, SC=1.0 / 32.0, nmac=2 * a + 2,
            qT_sb=qT_sb, kTm_sb=kTm_sb, vm_sb=vm_sb, p_sb=p_sb, pT_sb=pT_sb,
            O_sb=O_sb, stat=stat, dm_sb=dm_sb, eye_bf=eye_bf, pTt=pTt,
            kT_buf=kT_buf, v_buf=v_buf), st8)

        @block.vector
        def _(v):
            ve.wait(v)
            for t in range(4):
                ve.inc(v.reciprocal(stat[:, 12 + t:13 + t], stat[:, 4 + t:5 + t]))
            ve.wait(v)
            for t in range(4):
                ve.inc(v.tensor_scalar_mul(Obf_sb[:, t * 1024:(t + 1) * 1024],
                                           O_sb[:, t * 1024:(t + 1) * 1024],
                                           stat[:, 12 + t:13 + t]))

        ve_otb = ve.n
        pe_otb = pe.n

        @block.tensor
        def _(te):
            te.wait_ge(ve.h, ve_otb)
            for c in range(8):
                par = pTt[c % 2]
                if c >= 2:
                    te.wait_ge(ve.h, ve_otb + c - 1)
                for t in range(4):
                    inst = te.transpose(
                        par[:, t * 128:(t + 1) * 128],
                        Obf_sb[:, t * 1024 + c * 128: t * 1024 + (c + 1) * 128],
                        eye_bf[:, :])
                pe.inc(inst)

        @block.vector
        def _(v):
            for c in range(8):
                v.wait_ge(pe.h, pe_otb + c + 1)
                ve.inc(v.tensor_copy(oT_sb[:, c * 512:(c + 1) * 512],
                                     pTt[c % 2][:, :]))

        ld_res = ls.n
        ev_mark = {}
        for t in range(4):
            @block.tensor
            def _(te):
                if t == 0:
                    ve.wait(te)
                else:
                    te.wait_ge(ve.h, ev_mark[t - 1])
                for c in range(8):
                    for h in range(2):
                        inst = te.matmul(
                            pPV[:, h * 512:(h + 1) * 512],
                            oT_sb[:, c * 512 + t * 128: c * 512 + t * 128 + 128],
                            wo_sb[:, c * 1024 + h * 512: c * 1024 + (h + 1) * 512],
                            start=(c == 0), stop=(c == 7))
                pe.inc(inst)
            pv_t = pe.n

            @block.vector
            def _(v):
                v.wait_ge(pe.h, pv_t)
                v.wait_ge(ls.h, ld_res)
                ve.inc(v.tensor_tensor(res_sb[:, t * 1024:(t + 1) * 1024], pPV[:, :],
                                       xf_sb[:, t * 1024:(t + 1) * 1024], ALU.add))
            ev_mark[t] = ve.n

        for t in range(4):
            @block.vector
            def _(v):
                ve.wait(v)
                ve.inc(v.reduce_sum(ln_sb[:, t:t + 1],
                                    res_sb[:, t * 1024:(t + 1) * 1024], axis=AX.X))
                ve.wait(v)
                ve.inc(v.tensor_scalar_mul(ln_sb[:, 4 + t:5 + t], ln_sb[:, t:t + 1],
                                           -1.0 / 1024.0))
                ve.wait(v)
                ve.inc(v.tensor_scalar_add(res_sb[:, t * 1024:(t + 1) * 1024],
                                           res_sb[:, t * 1024:(t + 1) * 1024],
                                           ln_sb[:, 4 + t:5 + t]))

            @block.scalar
            def _(s):
                ve.wait(s)
                se.inc(s.activation(sq_sb[:, :], res_sb[:, t * 1024:(t + 1) * 1024],
                                    AF.Square, accum_out=ln_sb[:, t:t + 1]))

            @block.vector
            def _(v):
                se.wait(v)
                ve.inc(v.tensor_scalar(ln_sb[:, 8 + t:9 + t], ln_sb[:, t:t + 1],
                                       1.0 / 1024.0, LN_EPS, ALU.mult, ALU.add))

            @block.scalar
            def _(s):
                ve.wait(s)
                se.inc(s.activation(ln_sb[:, 8 + t:9 + t], ln_sb[:, 8 + t:9 + t],
                                    AF.Sqrt))

            @block.vector
            def _(v):
                se.wait(v)
                ve.inc(v.reciprocal(ln_sb[:, 12 + t:13 + t], ln_sb[:, 8 + t:9 + t]))
                ve.wait(v)
                ve.inc(v.tensor_scalar_mul(xo_sb[:, t * 1024:(t + 1) * 1024],
                                           res_sb[:, t * 1024:(t + 1) * 1024],
                                           ln_sb[:, 12 + t:13 + t]))

            @block.sync
            def _(sy):
                ve.wait(sy)
                so.dma(sy.dma_start(
                    out=x1q[a * 512 + t * 128: a * 512 + (t + 1) * 128, :],
                    in_=xo_sb[:, t * 1024:(t + 1) * 1024]))

            @block.gpsimd
            def _(g):
                ve.wait(g)
                so.dma(g.dma_start(
                    out=x1bf[a * 512 + t * 128: a * 512 + (t + 1) * 128, :],
                    in_=xo_sb[:, t * 1024:(t + 1) * 1024]))

    return stB



def emit_l0_ffn(E, io):
    nc, block = E.nc, E.block
    ld, ls, pe, ve, se, so, cc = E.ld, E.ls, E.pe, E.ve, E.se, E.so, E.cc
    pGU, pY, pT0 = E.pS, E.pPV, E.pT0
    xq, x1bf = io["x1q_buf"], io["x1bf"]
    sgT, suT, sdT = io["sgT"], io["suT"], io["sdT"]
    routerT, rbias = io["routerT"], io["rbias"]
    wgT, wuT, wdT = io["wgT"], io["wuT"], io["wdT"]
    eye, onehot = io["eye"], io["onehot"]
    x2q, x2bf, xg = io["x2q_buf"], io["x2bf"], io["xg"]
    wbg, wbu, wbd = io["wbg"], io["wbu"], io["wbd"]
    xe_all, ye_all = io["xe_all"], io["ye_all"]

    stB = ExitStack()
    pB = stB.enter_context(nc.psum_tensor("b2_pB", [128, 512], F32))
    eye_bf = stB.enter_context(nc.sbuf_tensor("b2_eyebf", [128, 128], BF16))
    eye_f = stB.enter_context(nc.sbuf_tensor("b2_eyef", [128, 128], F32))
    rT_sb = stB.enter_context(nc.sbuf_tensor("b2_rT", [128, 8 * 8], F32))
    oh_sb = stB.enter_context(nc.sbuf_tensor("b2_oh", [8, 8 * 128], F32))
    rb_sb = stB.enter_context(nc.sbuf_tensor("b2_rb", [1, 8], F32))
    rbbc_sb = stB.enter_context(nc.sbuf_tensor("b2_rbbc", [128, 8], F32))
    x_tm = stB.enter_context(nc.sbuf_tensor("b2_xtm", [128, 4 * 1024], BF16))
    xT = stB.enter_context(nc.sbuf_tensor("b2_xT", [128, 8 * 512], BF16))
    xf_sb = stB.enter_context(nc.sbuf_tensor("b2_xf", [128, 4 * 1024], F32))
    xc_sb = stB.enter_context(nc.sbuf_tensor("b2_xc", [128, 512], F32))
    wbuf = stB.enter_context(nc.sbuf_tensor("b2_wbuf", [128, 2 * 16384], BF16))
    wd_sb = stB.enter_context(nc.sbuf_tensor("b2_wd", [128, 16 * 1024], BF16))
    sg_sb = stB.enter_context(nc.sbuf_tensor("b2_sg", [128, 16 * 512], BF16))
    h_sb = sg_sb
    msk8 = stB.enter_context(nc.sbuf_tensor("b2_msk8", [8, 512], F32))
    cum8 = stB.enter_context(nc.sbuf_tensor("b2_cum8", [8, 512], F32))
    posx8 = cum8
    zs8 = stB.enter_context(nc.sbuf_tensor("b2_zs8", [8, 512], F32))
    ec_sb = stB.enter_context(nc.sbuf_tensor("b2_ec", [8, 1], F32))
    ovr8 = stB.enter_context(nc.sbuf_tensor("b2_ovr8", [8, 512], F32))
    posT_f = stB.enter_context(nc.sbuf_tensor("b2_posTf", [128, 32], F32))
    posT_i = stB.enter_context(nc.sbuf_tensor("b2_posTi", [128, 32], I32))
    xeT = stB.enter_context(nc.sbuf_tensor("b2_xeT", [128, 2 * 8 * CEXP], BF16))
    ye_sb = stB.enter_context(nc.sbuf_tensor("b2_ye", [128, 1024], BF16))
    ybuf = stB.enter_context(nc.sbuf_tensor("b2_ybuf", [128, 4 * 1024], BF16))
    zrow = stB.enter_context(nc.sbuf_tensor("b2_zrow", [128, 1024], BF16))
    acc_sb = stB.enter_context(nc.sbuf_tensor("b2_acc", [128, 4 * 1024], F32))
    logit_sb = stB.enter_context(nc.sbuf_tensor("b2_logit", [128, 32], F32))
    probs_sb = stB.enter_context(nc.sbuf_tensor("b2_probs", [128, 32], F32))
    gates_sb = stB.enter_context(nc.sbuf_tensor("b2_gates", [128, 32], F32))
    ge_sb = stB.enter_context(nc.sbuf_tensor("b2_ge", [128, 32], F32))
    tops_sb = stB.enter_context(nc.sbuf_tensor("b2_tops", [128, 8], F32))
    sp_sb = stB.enter_context(nc.sbuf_tensor("b2_sp", [128, 8], F32))
    rec_sb = stB.enter_context(nc.sbuf_tensor("b2_rec", [128, 4], F32))
    gT_sb = stB.enter_context(nc.sbuf_tensor("b2_gT", [8, 512], F32))
    ln_sb = stB.enter_context(nc.sbuf_tensor("b2_ln", [128, 16], F32))
    ones_sb = stB.enter_context(nc.sbuf_tensor("b2_ones", [1, 128], F32))

    @block.sync
    def _(sy):
        ls.dma(sy.dma_start(out=eye_f[:, :], in_=eye))
        ls.dma(sy.dma_start(out=oh_sb[:, :], in_=onehot))
        ls.dma(sy.dma_start(out=rb_sb[:, :], in_=rbias))
        ls.dma(sy.dma_start(out=rT_sb[:, :].rearrange("p (c e) -> p c e", c=8),
                            in_=routerT.rearrange("(c p) e -> p c e", p=128)))

    @block.gpsimd
    def _(g):
        ld.dma(g.dma_start(out=eye_bf[:, :], in_=eye))

    @block.sync
    def _(sy):
        ls.dma(sy.dma_start(out=ec_sb[:, :], in_=io["ecol"]))

    @block.vector
    def _(v):
        ve.inc(v.memset(zs8[:, :], 0.0))
        ve.inc(v.memset(ybuf[:, :], 0.0))
        ve.inc(v.memset(zrow[:, :], 0.0))
    ve_z = ve.n

    @block.gpsimd
    def _(g):
        g.wait_ge(ve.h, ve_z)
        for r in range(8 * NTE):
            ld.dma(g.dma_start(out=xe_all[r * 128:(r + 1) * 128, :],
                               in_=zrow[:, :]))

    E.b2_xeld = ls.n
    E.b2_peT = pe.n
    E.b2_gath = ld.n
    E.b2_xe_used = [pe.n, pe.n]
    E.b2_xe_cp = [ve.n, ve.n]
    E.b2_xt_used = pe.n
    cast_done = E.cast_done

    @block.vector
    def _(v):
        ve.inc(v.memset(ones_sb[:, :], 1.0))

    @block.tensor
    def _(te):
        ls.wait(te)
        ve.wait(te)
        pe.inc(te.matmul(pB[:, 0:8], ones_sb[:, :], rb_sb[:, :],
                         start=True, stop=True))

    @block.vector
    def _(v):
        pe.wait(v)
        ve.inc(v.tensor_copy(rbbc_sb[:, :], pB[:, 0:8]))

    for grp in range(NGRP):
        @block.gpsimd
        def _(g):
            ve.wait(g)
            se.wait(g)
            so.wait(g)
            for t in range(4):
                ld.dma(g.dma_start(
                    out=x_tm[:, t * 1024:(t + 1) * 1024],
                    in_=x1bf[grp * 512 + t * 128: grp * 512 + (t + 1) * 128, :]))
                ld.dma(g.dma_start(
                    out=xf_sb[:, t * 1024:(t + 1) * 1024],
                    in_=xq[grp * 512 + t * 128: grp * 512 + (t + 1) * 128, :]))
        ld_x = ld.n

        @block.sync
        def _(sy):
            sy.wait_ge(pe.h, E.b2_xt_used)
            if grp == 0:
                so.wait(sy)
            for c in range(8):
                ls.dma(sy.dma_start(
                    out=xT[:, c * 512:(c + 1) * 512],
                    in_=x1bf[grp * 512:(grp + 1) * 512, c * 128:(c + 1) * 128],
                    transpose=True))
        xt_ready = ls.n

        for c in range(8):
            ve_c = ve.n
            pe_c = pe.n

            @block.tensor
            def _(te):
                te.wait_ge(ve.h, ve_c)
                if c == 0:
                    te.wait_ge(ld.h, ld_x)
                for t in range(4):
                    inst = te.transpose(
                        pB[:, t * 128:(t + 1) * 128],
                        xf_sb[:, t * 1024 + c * 128: t * 1024 + (c + 1) * 128],
                        eye_f[:, :])
                pe.inc(inst)

            @block.vector
            def _(v):
                v.wait_ge(pe.h, pe_c + 1)
                ve.inc(v.tensor_copy(xc_sb[:, :], pB[:, :]))
            ve_c2 = ve.n

            @block.tensor
            def _(te):
                te.wait_ge(ve.h, ve_c2)
                for t in range(4):
                    inst = te.matmul(pGU[:, t * 512: t * 512 + 8],
                                     xc_sb[:, t * 128:(t + 1) * 128],
                                     rT_sb[:, c * 8:(c + 1) * 8],
                                     start=(c == 0), stop=(c == 7))
                pe.inc(inst)

        @block.vector
        def _(v):
            pe.wait(v)
            ve.inc(v.tensor_tensor(
                logit_sb[:, :].rearrange("p (a b) -> p a b", a=4),
                pGU[:, :].rearrange("p (a b) -> p a b", a=4)[:, :, 0:8],
                rbbc_sb[:, 0:8].rearrange("p (a b) -> p a b", a=1)
                    .broadcast_to([128, 4, 8]),
                ALU.add))
        ve_logit = ve.n

        @block.scalar
        def _(s):
            ve.wait(s)
            se.inc(s.activation(probs_sb[:, :], logit_sb[:, :], AF.Sigmoid))

        @block.vector
        def _(v):
            se.wait(v)
            for t in range(4):
                ve.inc(v.reduce_max(tops_sb[:, t:t + 1],
                                    logit_sb[:, t * 8:(t + 1) * 8], axis=AX.X))
            ve.wait(v)
            for t in range(4):
                ve.inc(v.tensor_scalar(ge_sb[:, t * 8:(t + 1) * 8],
                                       logit_sb[:, t * 8:(t + 1) * 8],
                                       tops_sb[:, t:t + 1], None, ALU.is_ge))
            ve.wait(v)
            for t in range(4):
                ve.inc(v.scalar_tensor_tensor(ge_sb[:, t * 8:(t + 1) * 8],
                                              ge_sb[:, t * 8:(t + 1) * 8], NEG,
                                              logit_sb[:, t * 8:(t + 1) * 8],
                                              ALU.mult, ALU.add))
            ve.wait(v)
            for t in range(4):
                ve.inc(v.reduce_max(tops_sb[:, 4 + t:5 + t],
                                    ge_sb[:, t * 8:(t + 1) * 8], axis=AX.X))

        @block.scalar
        def _(s):
            ve.wait(s)
            se.inc(s.activation(sp_sb[:, :], tops_sb[:, :], AF.Sigmoid))

        @block.vector
        def _(v):
            se.wait(v)
            for t in range(4):
                ve.inc(v.tensor_add(rec_sb[:, t:t + 1], sp_sb[:, t:t + 1],
                                    sp_sb[:, 4 + t:5 + t]))
            ve.wait(v)
            for t in range(4):
                ve.inc(v.reciprocal(rec_sb[:, t:t + 1], rec_sb[:, t:t + 1]))
            ve.wait(v)
            for t in range(4):
                ve.inc(v.tensor_scalar(ge_sb[:, t * 8:(t + 1) * 8],
                                       logit_sb[:, t * 8:(t + 1) * 8],
                                       tops_sb[:, 4 + t:5 + t], None, ALU.is_ge))
            ve.wait(v)
            for t in range(4):
                ve.inc(v.scalar_tensor_tensor(gates_sb[:, t * 8:(t + 1) * 8],
                                              probs_sb[:, t * 8:(t + 1) * 8],
                                              rec_sb[:, t:t + 1],
                                              ge_sb[:, t * 8:(t + 1) * 8],
                                              ALU.mult, ALU.mult))

        ve_g = ve.n
        pe_g = pe.n

        @block.tensor
        def _(te):
            te.wait_ge(ve.h, ve_g)
            for t in range(4):
                inst = te.transpose(pB[0:8, t * 128:(t + 1) * 128],
                                    gates_sb[:, t * 8:(t + 1) * 8], eye_f[:, :])
            pe.inc(inst)

        @block.vector
        def _(v):
            v.wait_ge(pe.h, pe_g + 1)
            ve.inc(v.tensor_copy(gT_sb[:, :], pB[0:8, :]))

        @block.vector
        def _(v):
            ve.wait(v)
            ve.inc(v.tensor_single_scalar(msk8[:, :], gT_sb[:, :], 0.0, ALU.is_gt))
            ve.wait(v)
            ve.inc(v.tensor_tensor_scan(cum8[:, :], msk8[:, :], zs8[:, :], 0.0,
                                        ALU.add, ALU.add))
            ve.wait(v)
            ve.inc(v.tensor_sub(cum8[:, :], cum8[:, :], msk8[:, :]))
            ve.wait(v)
            ve.inc(v.tensor_single_scalar(ovr8[:, :], cum8[:, :], float(CEXP),
                                          ALU.is_ge))
            ve.wait(v)
            ve.inc(v.tensor_scalar(cum8[:, :], cum8[:, :], ec_sb[:, 0:1], 1e6,
                                   ALU.add, ALU.add))
            ve.wait(v)
            ve.inc(v.scalar_tensor_tensor(posx8[:, :], msk8[:, :], -1e6,
                                          cum8[:, :], ALU.mult, ALU.add))
            ve.wait(v)
            ve.inc(v.scalar_tensor_tensor(posx8[:, :], ovr8[:, :], 2e6,
                                          posx8[:, :], ALU.mult, ALU.add))
        ve_pos = ve.n
        pe_pt0 = pe.n

        @block.tensor
        def _(te):
            te.wait_ge(ve.h, ve_pos)
            for t in range(4):
                inst = te.transpose(pB[:, t * 8:(t + 1) * 8],
                                    posx8[:, t * 128:(t + 1) * 128],
                                    eye_f[0:8, 0:8])
            pe.inc(inst)

        @block.vector
        def _(v):
            v.wait_ge(pe.h, pe_pt0 + 1)
            v.wait_ge(ld.h, E.b2_gath)
            ve.inc(v.tensor_copy(posT_f[:, :], pB[:, 0:32]))
            ve.wait(v)
            ve.inc(v.tensor_copy(posT_i[:, :], posT_f[:, :]))
        ve_pti = ve.n

        xeld_prev = E.b2_xeld

        @block.gpsimd
        def _(g):
            g.wait_ge(ve.h, ve_pti)
            g.wait_ge(ls.h, xeld_prev)
            for t in range(4):
                for e in range(8):
                    col = t * 8 + e
                    ld.dma(g.indirect_dma_start(
                        out=xe_all,
                        out_offset=bass.IndirectOffsetOnAxis(
                            posT_i[:, col:col + 1], 0),
                        in_=x_tm[:, t * 1024:(t + 1) * 1024], in_offset=None))
        scat_done = ld.n

        def emit_xe_prep(e):
            slot = e % 2
            xtb = slot * 8 * CEXP

            @block.sync
            def _(sy):
                sy.wait_ge(ld.h, scat_done)
                sy.wait_ge(pe.h, E.b2_xe_used[slot])
                for c in range(8):
                    ls.dma(sy.dma_start(
                        out=xeT[:, xtb + c * CEXP: xtb + (c + 1) * CEXP],
                        in_=xe_all[e * CEXP:(e + 1) * CEXP,
                                   c * 128:(c + 1) * 128],
                        transpose=True))
            E.b2_xe_cp[slot] = ls.n

        eseq = [8] + list(range(8))
        emit_xe_prep(0)
        if grp == 0:
            @block.sync
            def _(sy):
                sy.wait_ge(ld.h, cast_done[8])
                for c in range(8):
                    ls.dma(sy.dma_start(out=wbuf[:, c * 2048:(c + 1) * 2048],
                                        in_=wbg[8, c * 128:(c + 1) * 128, :]))
                    ls.dma(sy.dma_start(
                        out=wbuf[:, 16384 + c * 2048: 16384 + (c + 1) * 2048],
                        in_=wbu[8, c * 128:(c + 1) * 128, :]))
            E.b2_ld_gu = ls.n

        for ei, e in enumerate(eseq):
            routed = (e != 8)
            W = CEXP if routed else 512
            pe_e0 = pe.n
            ld_gu = E.b2_ld_gu

            if routed:
                slot = e % 2
                rhs_buf = xeT[:, slot * 8 * CEXP:(slot + 1) * 8 * CEXP]
                rhs_w = CEXP
                xe_cp = E.b2_xe_cp[slot]
            else:
                rhs_buf, rhs_w = xT, 512
                xe_cp = xt_ready

            se_gb = se.n
            pe_gb = pe.n
            ve_gu_guard = ve_logit if ei == 0 else ve.n

            @block.tensor
            def _(te):
                te.wait_ge(ls.h, ld_gu)
                se.wait(te)
                te.wait_ge(ve.h, ve_gu_guard)
                te.wait_ge(ls.h, xe_cp)
                for oc in range(16):
                    if oc >= 2:
                        te.wait_ge(se.h, se_gb + oc - 1)
                    for c in range(8):
                        inst = te.matmul(
                            pGU[:, (oc % 2) * 512: (oc % 2) * 512 + W],
                            wbuf[:, c * 2048 + oc * 128: c * 2048 + oc * 128 + 128],
                            rhs_buf[:, c * rhs_w:(c + 1) * rhs_w],
                            start=(c == 0), stop=(c == 7))
                    pe.inc(inst)

            @block.sync
            def _(sy):
                sy.wait_ge(ld.h, cast_done[e])
                sy.wait_ge(pe.h, pe_e0)
                for i in range(16):
                    ls.dma(sy.dma_start(out=wd_sb[:, i * 1024:(i + 1) * 1024],
                                        in_=wbd[e, i * 128:(i + 1) * 128, :]))
            ld_d = ls.n

            @block.scalar
            def _(s):
                for oc in range(16):
                    s.wait_ge(pe.h, pe_gb + oc + 1)
                    se.inc(s.activation(sg_sb[:, oc * W:(oc + 1) * W],
                                        pGU[:, (oc % 2) * 512: (oc % 2) * 512 + W],
                                        AF.Silu))
            se_gu = se.n

            ve_ub = ve.n
            pe_ub = pe.n

            @block.tensor
            def _(te):
                te.wait_ge(ve.h, ve_gu_guard if ei == 0 else ve_ub)
                for oc in range(16):
                    if oc >= 2:
                        te.wait_ge(ve.h, ve_ub + oc - 1)
                    for c in range(8):
                        inst = te.matmul(
                            pGU[:, (2 + oc % 2) * 512: (2 + oc % 2) * 512 + W],
                            wbuf[:, 16384 + c * 2048 + oc * 128:
                                 16384 + c * 2048 + oc * 128 + 128],
                            rhs_buf[:, c * rhs_w:(c + 1) * rhs_w],
                            start=(c == 0), stop=(c == 7))
                    pe.inc(inst)
            pe_after_gu = pe.n
            if routed:
                E.b2_xe_used[e % 2] = pe.n
                if e + 1 <= 7:
                    emit_xe_prep(e + 1)
            else:
                E.b2_xt_used = pe.n

            nxt = None
            if ei + 1 < len(eseq):
                nxt = eseq[ei + 1]
            elif grp + 1 < NGRP:
                nxt = 8
            if nxt is not None:
                @block.sync
                def _(sy):
                    sy.wait_ge(ld.h, cast_done[nxt])
                    sy.wait_ge(pe.h, pe_after_gu)
                    for c in range(8):
                        ls.dma(sy.dma_start(out=wbuf[:, c * 2048:(c + 1) * 2048],
                                            in_=wbg[nxt, c * 128:(c + 1) * 128, :]))
                        ls.dma(sy.dma_start(
                            out=wbuf[:, 16384 + c * 2048: 16384 + (c + 1) * 2048],
                            in_=wbu[nxt, c * 128:(c + 1) * 128, :]))
                E.b2_ld_gu = ls.n

            @block.vector
            def _(v):
                v.wait_ge(se.h, se_gu)
                for oc in range(16):
                    v.wait_ge(pe.h, pe_ub + oc + 1)
                    ve.inc(v.tensor_tensor(
                        h_sb[:, oc * W:(oc + 1) * W],
                        pGU[:, (2 + oc % 2) * 512: (2 + oc % 2) * 512 + W],
                        sg_sb[:, oc * W:(oc + 1) * W],
                        ALU.mult))
            ve_hb = ve.n

            if routed:
                for tt in range(NTE):
                    ve_pre = ve.n

                    @block.tensor
                    def _(te):
                        te.wait_ge(ls.h, ld_d)
                        te.wait_ge(ve.h, ve_pre)
                        for i in range(16):
                            for hh in range(2):
                                inst = te.matmul(
                                    pY[:, hh * 512:(hh + 1) * 512],
                                    h_sb[:, i * CEXP + tt * 128:
                                         i * CEXP + tt * 128 + 128],
                                    wd_sb[:, i * 1024 + hh * 512:
                                          i * 1024 + (hh + 1) * 512],
                                    start=(i == 0), stop=(i == 15))
                        pe.inc(inst)
                    pv_tt = pe.n

                    @block.vector
                    def _(v):
                        v.wait_ge(pe.h, pv_tt)
                        if tt > 0:
                            so.wait(v)
                        ve.inc(v.tensor_copy(ye_sb[:, :], pY[:, :]))
                    ve_evac = ve.n

                    @block.sync
                    def _(sy):
                        sy.wait_ge(ve.h, ve_evac)
                        so.dma(sy.dma_start(
                            out=ye_all[e * CEXP + tt * 128:
                                       e * CEXP + (tt + 1) * 128, :],
                            in_=ye_sb[:, :]))
                so_y = so.n

                ve_accprev = ve.n

                @block.gpsimd
                def _(g):
                    g.wait_ge(so.h, so_y)
                    g.wait_ge(ve.h, ve_accprev)
                    for t in range(4):
                        col = t * 8 + e
                        ld.dma(g.indirect_dma_start(
                            out=ybuf[:, t * 1024:(t + 1) * 1024], out_offset=None,
                            in_=ye_all,
                            in_offset=bass.IndirectOffsetOnAxis(
                                posT_i[:, col:col + 1], 0)))
                ld_gath = ld.n
                E.b2_gath = ld.n

                @block.vector
                def _(v):
                    v.wait_ge(ld.h, ld_gath)
                    for t in range(4):
                        col = t * 8 + e
                        ve.inc(v.scalar_tensor_tensor(
                            acc_sb[:, t * 1024:(t + 1) * 1024],
                            ybuf[:, t * 1024:(t + 1) * 1024],
                            gates_sb[:, col:col + 1],
                            acc_sb[:, t * 1024:(t + 1) * 1024],
                            ALU.mult, ALU.add))
            else:
                ve_yb = ve.n
                pe_yb = pe.n
                for t in range(4):
                    @block.tensor
                    def _(te):
                        te.wait_ge(ls.h, ld_d)
                        te.wait_ge(ve.h, ve_hb)
                        if t > 0:
                            te.wait_ge(ve.h, ve_yb + t)
                        for i in range(16):
                            for hh in range(2):
                                inst = te.matmul(
                                    pY[:, hh * 512:(hh + 1) * 512],
                                    h_sb[:, i * 512 + t * 128: i * 512 + t * 128 + 128],
                                    wd_sb[:, i * 1024 + hh * 512:
                                          i * 1024 + (hh + 1) * 512],
                                    start=(i == 0), stop=(i == 15))
                        pe.inc(inst)

                    @block.vector
                    def _(v):
                        v.wait_ge(pe.h, pe_yb + t + 1)
                        ve.inc(v.tensor_copy(acc_sb[:, t * 1024:(t + 1) * 1024],
                                             pY[:, :]))


        for t in range(4):
            @block.vector
            def _(v):
                ve.wait(v)
                ve.inc(v.tensor_tensor(acc_sb[:, t * 1024:(t + 1) * 1024],
                                       acc_sb[:, t * 1024:(t + 1) * 1024],
                                       xf_sb[:, t * 1024:(t + 1) * 1024], ALU.add))
                ve.wait(v)
                ve.inc(v.reduce_sum(ln_sb[:, t:t + 1],
                                    acc_sb[:, t * 1024:(t + 1) * 1024], axis=AX.X))
                ve.wait(v)
                ve.inc(v.tensor_scalar_mul(ln_sb[:, 4 + t:5 + t], ln_sb[:, t:t + 1],
                                           -1.0 / 1024.0))
                ve.wait(v)
                ve.inc(v.tensor_scalar_add(acc_sb[:, t * 1024:(t + 1) * 1024],
                                           acc_sb[:, t * 1024:(t + 1) * 1024],
                                           ln_sb[:, 4 + t:5 + t]))

            @block.scalar
            def _(s):
                ve.wait(s)
                se.inc(s.activation(x_tm[:, 0:1024],
                                    acc_sb[:, t * 1024:(t + 1) * 1024],
                                    AF.Square, accum_out=ln_sb[:, t:t + 1]))

            @block.vector
            def _(v):
                se.wait(v)
                ve.inc(v.tensor_scalar(ln_sb[:, 8 + t:9 + t], ln_sb[:, t:t + 1],
                                       1.0 / 1024.0, LN_EPS, ALU.mult, ALU.add))

            @block.scalar
            def _(s):
                ve.wait(s)
                se.inc(s.activation(ln_sb[:, 8 + t:9 + t], ln_sb[:, 8 + t:9 + t],
                                    AF.Sqrt))

            @block.vector
            def _(v):
                se.wait(v)
                ve.inc(v.reciprocal(ln_sb[:, 12 + t:13 + t], ln_sb[:, 8 + t:9 + t]))
                ve.wait(v)
                ve.inc(v.tensor_scalar_mul(xf_sb[:, t * 1024:(t + 1) * 1024],
                                           acc_sb[:, t * 1024:(t + 1) * 1024],
                                           ln_sb[:, 12 + t:13 + t]))

            @block.sync
            def _(sy):
                ve.wait(sy)
                so.dma(sy.dma_start(
                    out=x2q[grp * 512 + t * 128: grp * 512 + (t + 1) * 128, :],
                    in_=xf_sb[:, t * 1024:(t + 1) * 1024]))

            @block.gpsimd
            def _(g):
                ve.wait(g)
                so.dma(g.dma_start(
                    out=x2bf[grp * 512 + t * 128: grp * 512 + (t + 1) * 128, :],
                    in_=xf_sb[:, t * 1024:(t + 1) * 1024]))

        @block.gpsimd
        def _(g):
            so.wait(g)
            E.cc.inc(g.collective_compute(
                "AllGather", ALU.bypass, replica_groups=[list(range(NCORE))],
                ins=[x2bf[grp * 512:(grp + 1) * 512, :].opt()],
                outs=[xg[grp * NCORE * 512:(grp + 1) * NCORE * 512, :].opt()],
                output_never_consumed=True), 1)

    return stB



def emit_l1_attn(E, io):
    nc, block = E.nc, E.block
    ld, ls, pe, ve, se, so, cc = E.ld, E.ls, E.pe, E.ve, E.se, E.so, E.cc
    cg = E.cg
    pS, pPV, pT0 = E.pS, E.pPV, E.pT0
    xg, xgb, xq = io["xg"], io["xgb"], io["x2q_buf"]
    x2bf, x3bf = io["x2bf"], io["x3bf"]
    WqT, WkT, WvT = io["WqT"], io["WkT"], io["WvT"]
    W1T, W2T, W3T = io["W1T"], io["W2T"], io["W3T"]
    bq_pc, bk_pc = io["bq_pc"], io["bk_pc"]
    b1_pc, b2_pc = io["b1_pc"], io["b2_pc"]
    bv_row, b3_row = io["bv_row"], io["b3_row"]
    eye = io["eye"]
    x3q = io["x3q_buf"]
    kT_buf, v_buf, qT_buf = io["kT_buf"], io["v_buf"], io["qT_buf"]
    w1bf, w2bf, w3bf = io["w1bf"], io["w2bf"], io["w3bf"]
    cc_base = cc.n - NGRP

    stB = ExitStack()
    pT1 = stB.enter_context(nc.psum_tensor("b3_pT1", [128, 512], BF16))
    pTt = [pT0, pT1]
    eye_bf = stB.enter_context(nc.sbuf_tensor("b3_eye", [128, 128], BF16))
    stat = stB.enter_context(nc.sbuf_tensor("b3_stat", [128, 24], F32))
    ln_sb = stB.enter_context(nc.sbuf_tensor("b3_ln", [128, 16], F32))
    ones_r = stB.enter_context(nc.sbuf_tensor("b3_onesr", [1, 128], F32))
    ones_c = stB.enter_context(nc.sbuf_tensor("b3_onesc", [128, 1], F32))
    b1s = stB.enter_context(nc.sbuf_tensor("b3_b1s", [128, 8], F32))
    b2s = stB.enter_context(nc.sbuf_tensor("b3_b2s", [128, 32], F32))
    b3r = stB.enter_context(nc.sbuf_tensor("b3_b3r", [1, D], F32))
    b3b = stB.enter_context(nc.sbuf_tensor("b3_b3b", [128, D], BF16))

    stA = ExitStack()
    wq_sb = stA.enter_context(nc.sbuf_tensor("b3_wq", [128, 8 * 1024], BF16))
    wk_sb = stA.enter_context(nc.sbuf_tensor("b3_wk", [128, 8 * 1024], BF16))
    wv_sb = stA.enter_context(nc.sbuf_tensor("b3_wv", [128, 8 * 1024], BF16))
    xT = stA.enter_context(nc.sbuf_tensor("b3_xT", [128, 2 * 8 * 512], BF16))
    kf_sb = stA.enter_context(nc.sbuf_tensor("b3_kf", [128, 8 * 512], F32))
    ksq_sb = stA.enter_context(nc.sbuf_tensor("b3_ksq", [128, 2 * 512], F32))
    rk_row = stA.enter_context(nc.sbuf_tensor("b3_rk", [1, 512], F32))
    rkb_sb = stA.enter_context(nc.sbuf_tensor("b3_rkb", [128, 512], F32))
    kv_stage = stA.enter_context(nc.sbuf_tensor("b3_kvst", [128, 8 * 512], BF16))
    v_stage = stA.enter_context(nc.sbuf_tensor("b3_vst", [128, 4 * 1024], BF16))
    bq_s = stA.enter_context(nc.sbuf_tensor("b3_bqs", [128, 8], F32))
    bk_s = stA.enter_context(nc.sbuf_tensor("b3_bks", [128, 8], F32))
    bv_r = stA.enter_context(nc.sbuf_tensor("b3_bvr", [1, D], F32))
    bvb_sb = stA.enter_context(nc.sbuf_tensor("b3_bvb", [128, D], F32))

    @block.gpsimd
    def _(g):
        E.bsel = g.partition_id() // GPB
        for w_sb, src in ((wq_sb, WqT), (wk_sb, WkT), (wv_sb, WvT)):
            for c in range(8):
                ld.dma(g.dma_start(out=w_sb[:, c * 1024:(c + 1) * 1024],
                                   in_=src[c * 128:(c + 1) * 128, :]))
        ld.dma(g.dma_start(out=eye_bf[:, :], in_=eye))
    mlp_cast = E.mlp_cast

    @block.sync
    def _(sy):
        ls.dma(sy.dma_start(out=bq_s[:, :], in_=bq_pc))
        ls.dma(sy.dma_start(out=bk_s[:, :], in_=bk_pc))
        ls.dma(sy.dma_start(out=b1s[:, :], in_=b1_pc))
        ls.dma(sy.dma_start(out=b2s[:, :], in_=b2_pc))
        ls.dma(sy.dma_start(out=bv_r[:, :], in_=bv_row))
        ls.dma(sy.dma_start(out=b3r[:, :], in_=b3_row))

    @block.vector
    def _(v):
        ve.inc(v.memset(ones_r[:, :], 1.0))
        ve.inc(v.memset(ones_c[:, :], 1.0))

    @block.tensor
    def _(te):
        ls.wait(te)
        ve.wait(te)
        for h in range(2):
            pe.inc(te.matmul(pS[:, h * 512:(h + 1) * 512], ones_r[:, :],
                             bv_r[:, h * 512:(h + 1) * 512], start=True, stop=True))
        for h in range(2):
            pe.inc(te.matmul(pS[:, 1024 + h * 512: 1024 + (h + 1) * 512],
                             ones_r[:, :], b3r[:, h * 512:(h + 1) * 512],
                             start=True, stop=True))
    pe_bc = pe.n

    @block.vector
    def _(v):
        v.wait_ge(pe.h, pe_bc)
        ve.inc(v.tensor_copy(bvb_sb[:, :], pS[:, 0:1024]))
        ve.inc(v.tensor_copy(b3b[:, :], pS[:, 1024:2048]))

    xt_used = [pe.n, pe.n]

    def emit_group_xT(src_bf, grp, gathered=False, so_gate=False):
        slot = grp % 2
        base = slot * 4096
        guard = xt_used[slot]

        @block.sync
        def _(sy):
            sy.wait_ge(pe.h, guard)
            if gathered:
                cg.wait(sy, 16 * (grp // 4 + 1))
            if so_gate:
                so.wait(sy)
            for c in range(8):
                ls.dma(sy.dma_start(
                    out=xT[:, base + c * 512: base + (c + 1) * 512],
                    in_=src_bf[grp * 512:(grp + 1) * 512, c * 128:(c + 1) * 128],
                    transpose=True))
        return base, ls.n

    def emit_norm_proj(w_sb, bias_s, dst_buf, grp, xbase, xready):
        ve_now = ve.n
        se_base = se.n
        pe_base = pe.n

        @block.tensor
        def _(te):
            te.wait_ge(ls.h, xready)
            te.wait_ge(ve.h, ve_now)
            te.wait_ge(se.h, se_base)
            for oc in range(8):
                if oc >= 3:
                    te.wait_ge(se.h, se_base + oc - 2)
                for c in range(8):
                    inst = te.matmul(
                        pS[:, (oc % 3) * 512:(oc % 3 + 1) * 512],
                        w_sb[:, c * 1024 + oc * 128: c * 1024 + oc * 128 + 128],
                        xT[:, xbase + c * 512:(c + 1) * 512 + xbase],
                        start=(c == 0), stop=(c == 7))
                pe.inc(inst)

        @block.scalar
        def _(s):
            for oc in range(8):
                s.wait_ge(pe.h, pe_base + oc + 1)
                se.inc(s.activation(kf_sb[:, oc * 512:(oc + 1) * 512],
                                    pS[:, (oc % 3) * 512:(oc % 3 + 1) * 512],
                                    AF.Identity, bias=bias_s[:, oc:oc + 1]))
        se_evac = se.n

        pe_sq = pe.n
        se_sq0 = se.n

        @block.scalar
        def _(s):
            s.wait_ge(se.h, se_evac)
            for oc in range(8):
                if oc >= 2:
                    s.wait_ge(pe.h, pe_sq + oc - 1)
                se.inc(s.activation(ksq_sb[:, (oc % 2) * 512:(oc % 2 + 1) * 512],
                                    kf_sb[:, oc * 512:(oc + 1) * 512], AF.Square))

        @block.tensor
        def _(te):
            for oc in range(8):
                te.wait_ge(se.h, se_sq0 + oc + 1)
                inst = te.matmul(pS[0:1, 3 * 512:3 * 512 + 512],
                                 ones_c[:, :],
                                 ksq_sb[:, (oc % 2) * 512:(oc % 2 + 1) * 512],
                                 start=(oc == 0), stop=(oc == 7))
                pe.inc(inst)
        pe_ssq = pe.n

        @block.vector
        def _(v):
            v.wait_ge(pe.h, pe_ssq)
            ve.inc(v.tensor_copy(rk_row[:, :], pS[0:1, 3 * 512:3 * 512 + 512]))

        @block.scalar
        def _(s):
            ve.wait(s)
            se.inc(s.activation(rk_row[:, :], rk_row[:, :], AF.Sqrt))

        @block.vector
        def _(v):
            se.wait(v)
            ve.inc(v.tensor_scalar_max(rk_row[:, :], rk_row[:, :], 1e-12))
            ve.wait(v)
            ve.inc(v.reciprocal(rk_row[:, :], rk_row[:, :]))
        ve_rk = ve.n

        @block.tensor
        def _(te):
            te.wait_ge(ve.h, ve_rk)
            pe.inc(te.matmul(pS[:, 3 * 512:4 * 512], ones_r[:, :], rk_row[:, :],
                             start=True, stop=True))
        pe_rkb = pe.n

        @block.vector
        def _(v):
            v.wait_ge(pe.h, pe_rkb)
            ve.inc(v.tensor_copy(rkb_sb[:, :], pS[:, 3 * 512:4 * 512]))
            ve.wait(v)
            for oc in range(8):
                ve.inc(v.tensor_tensor(kv_stage[:, oc * 512:(oc + 1) * 512],
                                       kf_sb[:, oc * 512:(oc + 1) * 512],
                                       rkb_sb[:, :], ALU.mult))
        ve_st = ve.n

        @block.sync
        def _(sy):
            sy.wait_ge(ve.h, ve_st)
            for oc in range(8):
                so.dma(sy.dma_start(
                    out=dst_buf[oc * 128:(oc + 1) * 128, grp * 512:(grp + 1) * 512],
                    in_=kv_stage[:, oc * 512:(oc + 1) * 512]))

    def stage_chunk(chunk):
        @block.gpsimd
        def _(g):
            g.wait_ge(cc.h, cc_base + chunk + 1)
            cg.dma(g.dma_start(
                out=xgb[chunk * 2048:(chunk + 1) * 2048, :],
                in_=xg[bass.ts(E.bsel + 2 * chunk, 2048), :]))

    stage_chunk(0)
    nfo = emit_group_xT(xgb, 0, gathered=True)
    for grp in range(32):
        xbase, xready = nfo
        if grp + 1 < 32:
            if (grp + 1) % 4 == 0:
                stage_chunk((grp + 1) // 4)
            nfo = emit_group_xT(xgb, grp + 1, gathered=True)
        else:
            nfo = emit_group_xT(x2bf, 0, so_gate=True)
        emit_norm_proj(wk_sb, bk_s, kT_buf, grp, xbase, xready)

        ve_vbase = ve.n
        pe_vbase = pe.n
        for t in range(4):
            @block.tensor
            def _(te):
                if t == 0:
                    te.wait_ge(ve.h, ve_vbase)
                else:
                    te.wait_ge(ve.h, ve_vbase + t)
                for c in range(8):
                    for h in range(2):
                        inst = te.matmul(
                            pPV[:, h * 512:(h + 1) * 512],
                            xT[:, xbase + c * 512 + t * 128:
                               xbase + c * 512 + t * 128 + 128],
                            wv_sb[:, c * 1024 + h * 512: c * 1024 + (h + 1) * 512],
                            start=(c == 0), stop=(c == 7))
                pe.inc(inst)

            @block.vector
            def _(v):
                v.wait_ge(pe.h, pe_vbase + t + 1)
                ve.inc(v.tensor_tensor(v_stage[:, t * 1024:(t + 1) * 1024],
                                       pPV[:, :], bvb_sb[:, :], ALU.add))

        xt_used[grp % 2] = pe.n
        ve_v = ve.n

        @block.sync
        def _(sy):
            sy.wait_ge(ve.h, ve_v)
            for t in range(4):
                so.dma(sy.dma_start(
                    out=v_buf[grp * 512 + t * 128: grp * 512 + (t + 1) * 128, :],
                    in_=v_stage[:, t * 1024:(t + 1) * 1024]))

        @block.tensor
        def _(te):
            so.wait(te)

    for grp in range(8):
        xbase, xready = nfo
        if grp + 1 < 8:
            nfo = emit_group_xT(x2bf, grp + 1)
        emit_norm_proj(wq_sb, bq_s, qT_buf, grp, xbase, xready)
        xt_used[grp % 2] = pe.n

        @block.tensor
        def _(te):
            so.wait(te)

    stA.close()
    qT_sb = stB.enter_context(nc.sbuf_tensor("b3_qT", [128, 8 * 512], BF16))
    kTm_sb = stB.enter_context(nc.sbuf_tensor("b3_kTm", [128, 8 * 2048], BF16))
    vm_sb = stB.enter_context(nc.sbuf_tensor("b3_vm", [128, 16 * 1024], BF16))
    p_sb = stB.enter_context(nc.sbuf_tensor("b3_p", [128, 2048], BF16))
    pT_sb = stB.enter_context(nc.sbuf_tensor("b3_pT", [128, 16 * 128], BF16))
    O_sb = stB.enter_context(nc.sbuf_tensor("b3_O", [128, 4 * 1024], F32))
    Obf_sb = stB.enter_context(nc.sbuf_tensor("b3_Obf", [128, 4 * 1024], BF16))
    wbuf = stB.enter_context(nc.sbuf_tensor("b3_wbuf", [128, 32 * 1024], BF16))
    w1_sb = stB.enter_context(nc.sbuf_tensor("b3_w1", [128, 8 * 1024], BF16))
    xf_sb = stB.enter_context(nc.sbuf_tensor("b3_xf", [128, 4 * 1024], F32))
    if "dbg_attn" in io:
        dst_sb = stB.enter_context(nc.sbuf_tensor("b3_dst", [128, 24], F32))

    st8 = dict(exp=[se.n, se.n], sts=[ve.n, ve.n], ptdone=[pe.n, pe.n],
               pvdone=[pe.n, pe.n], oup=ve.n, odone=ve.n, kvfree=[pe.n, pe.n],
               kvready=[ls.n, ls.n])
    for a in range(NGRP):
        @block.sync
        def _(sy):
            pe.wait(sy)
            so.wait(sy)
            for c in range(8):
                ls.dma(sy.dma_start(
                    out=qT_sb[:, c * 512:(c + 1) * 512],
                    in_=qT_buf[c * 128:(c + 1) * 128, a * 512:(a + 1) * 512]))

        @block.sync
        def _(sy):
            ve.wait(sy)
            se.wait(sy)
            so.wait(sy)
            for t in range(4):
                ls.dma(sy.dma_start(
                    out=xf_sb[:, t * 1024:(t + 1) * 1024],
                    in_=xq[a * 512 + t * 128: a * 512 + (t + 1) * 128, :]))
        ls_res = ls.n

        pe_wpre = pe.n

        @block.sync
        def _(sy):
            sy.wait_ge(ld.h, mlp_cast)
            sy.wait_ge(pe.h, pe_wpre)
            for c in range(8):
                ls.dma(sy.dma_start(out=w1_sb[:, c * 1024:(c + 1) * 1024],
                                    in_=w1bf[c * 128:(c + 1) * 128, :]))
            for c in range(8):
                ls.dma(sy.dma_start(out=wbuf[:, c * 4096:(c + 1) * 4096],
                                    in_=w2bf[c * 128:(c + 1) * 128, :]))
        ld_w12 = ls.n

        emit_flash_group(E, a, dict(
            causal=False, SC=1.0, nmac=16,
            qT_sb=qT_sb, kTm_sb=kTm_sb, vm_sb=vm_sb, p_sb=p_sb, pT_sb=pT_sb,
            O_sb=O_sb, stat=stat, dm_sb=None, eye_bf=eye_bf, pTt=pTt,
            kT_buf=kT_buf, v_buf=v_buf), st8)

        @block.vector
        def _(v):
            ve.wait(v)
            se.wait(v)
            for t in range(4):
                ve.inc(v.reciprocal(stat[:, 12 + t:13 + t], stat[:, 4 + t:5 + t]))
            ve.wait(v)
            for t in range(4):
                ve.inc(v.tensor_scalar_mul(Obf_sb[:, t * 1024:(t + 1) * 1024],
                                           O_sb[:, t * 1024:(t + 1) * 1024],
                                           stat[:, 12 + t:13 + t]))

        if "dbg_attn" in io:
            @block.vector
            def _(v):
                ve.wait(v)
                ve.inc(v.tensor_copy(dst_sb[:, :], stat[:, :]))
            ve_dx = ve.n

            @block.sync
            def _(sy):
                sy.wait_ge(ve.h, ve_dx)
                for t in range(4):
                    so.dma(sy.dma_start(
                        out=io["dbg_attn"][a * 512 + t * 128:
                                           a * 512 + (t + 1) * 128, :],
                        in_=Obf_sb[:, t * 1024:(t + 1) * 1024]))
                so.dma(sy.dma_start(out=io["dbg_stat"][:, a * 24:(a + 1) * 24],
                                    in_=dst_sb[:, :]))

        ve_otb = ve.n
        pe_otb = pe.n

        @block.tensor
        def _(te):
            te.wait_ge(ve.h, ve_otb)
            for c in range(8):
                par = pTt[c % 2]
                if c >= 2:
                    te.wait_ge(ve.h, ve_otb + c - 1)
                for t in range(4):
                    inst = te.transpose(
                        par[:, t * 128:(t + 1) * 128],
                        Obf_sb[:, t * 1024 + c * 128: t * 1024 + (c + 1) * 128],
                        eye_bf[:, :])
                pe.inc(inst)

        @block.vector
        def _(v):
            for c in range(8):
                v.wait_ge(pe.h, pe_otb + c + 1)
                ve.inc(v.tensor_copy(vm_sb[:, 4096 + c * 512: 4096 + (c + 1) * 512],
                                     pTt[c % 2][:, :]))

        ve_h1 = ve.n
        se_h1b = se.n
        pe_h1b = pe.n

        @block.tensor
        def _(te):
            te.wait_ge(ls.h, ld_w12)
            te.wait_ge(ve.h, ve_h1)
            se.wait(te)
            for oc in range(8):
                if oc >= 4:
                    te.wait_ge(se.h, se_h1b + oc - 3)
                for c in range(8):
                    inst = te.matmul(
                        pS[:, (oc % 4) * 512:(oc % 4 + 1) * 512],
                        w1_sb[:, c * 1024 + oc * 128: c * 1024 + oc * 128 + 128],
                        vm_sb[:, 4096 + c * 512: 4096 + (c + 1) * 512],
                        start=(c == 0), stop=(c == 7))
                pe.inc(inst)

        @block.scalar
        def _(s):
            for oc in range(8):
                s.wait_ge(pe.h, pe_h1b + oc + 1)
                se.inc(s.activation(vm_sb[:, oc * 512:(oc + 1) * 512],
                                    pS[:, (oc % 4) * 512:(oc % 4 + 1) * 512],
                                    AF.Relu, bias=b1s[:, oc:oc + 1]))

        se_h2b = se.n
        pe_h2b = pe.n
        se_h1done = se.n

        @block.tensor
        def _(te):
            te.wait_ge(ls.h, ld_w12)
            te.wait_ge(se.h, se_h1done)
            for oc in range(32):
                if oc >= 4:
                    te.wait_ge(se.h, se_h2b + oc - 3)
                for c in range(8):
                    inst = te.matmul(
                        pS[:, (oc % 4) * 512:(oc % 4 + 1) * 512],
                        wbuf[:, c * 4096 + oc * 128: c * 4096 + oc * 128 + 128],
                        vm_sb[:, c * 512:(c + 1) * 512],
                        start=(c == 0), stop=(c == 7))
                pe.inc(inst)

        @block.scalar
        def _(s):
            for oc in range(32):
                s.wait_ge(pe.h, pe_h2b + oc + 1)
                se.inc(s.activation(kTm_sb[:, oc * 512:(oc + 1) * 512],
                                    pS[:, (oc % 4) * 512:(oc % 4 + 1) * 512],
                                    AF.Relu, bias=b2s[:, oc:oc + 1]))

        pe_w3g = pe.n

        @block.sync
        def _(sy):
            sy.wait_ge(pe.h, pe_w3g)
            for hc in range(32):
                ls.dma(sy.dma_start(out=wbuf[:, hc * 1024:(hc + 1) * 1024],
                                    in_=w3bf[hc * 128:(hc + 1) * 128, :]))
        ld_w3 = ls.n
        se_h2done = se.n
        ld_res = ls_res

        ev_mark = {}
        for t in range(4):
            @block.tensor
            def _(te):
                te.wait_ge(ls.h, ld_w3)
                te.wait_ge(se.h, se_h2done)
                if t == 0:
                    ve.wait(te)
                else:
                    te.wait_ge(ve.h, ev_mark[t - 1])
                for hc in range(32):
                    for h in range(2):
                        inst = te.matmul(
                            pPV[:, h * 512:(h + 1) * 512],
                            kTm_sb[:, hc * 512 + t * 128: hc * 512 + t * 128 + 128],
                            wbuf[:, hc * 1024 + h * 512: hc * 1024 + (h + 1) * 512],
                            start=(hc == 0), stop=(hc == 31))
                pe.inc(inst)
            pv_t = pe.n

            @block.vector
            def _(v):
                v.wait_ge(pe.h, pv_t)
                v.wait_ge(ls.h, ld_res)
                ve.inc(v.tensor_tensor(O_sb[:, t * 1024:(t + 1) * 1024], pPV[:, :],
                                       b3b[:, :], ALU.add))
            ev_mark[t] = ve.n

        for t in range(4):
            @block.vector
            def _(v):
                ve.wait(v)
                ve.inc(v.tensor_tensor(O_sb[:, t * 1024:(t + 1) * 1024],
                                       O_sb[:, t * 1024:(t + 1) * 1024],
                                       xf_sb[:, t * 1024:(t + 1) * 1024], ALU.add))
                ve.wait(v)
                ve.inc(v.reduce_sum(ln_sb[:, t:t + 1],
                                    O_sb[:, t * 1024:(t + 1) * 1024], axis=AX.X))
                ve.wait(v)
                ve.inc(v.tensor_scalar_mul(ln_sb[:, 4 + t:5 + t], ln_sb[:, t:t + 1],
                                           -1.0 / 1024.0))
                ve.wait(v)
                ve.inc(v.tensor_scalar_add(O_sb[:, t * 1024:(t + 1) * 1024],
                                           O_sb[:, t * 1024:(t + 1) * 1024],
                                           ln_sb[:, 4 + t:5 + t]))

            @block.scalar
            def _(s):
                ve.wait(s)
                if t == 0 and "dbg_attn" in io:
                    so.wait(s)
                se.inc(s.activation(Obf_sb[:, 0:1024],
                                    O_sb[:, t * 1024:(t + 1) * 1024],
                                    AF.Square, accum_out=ln_sb[:, t:t + 1]))

            @block.vector
            def _(v):
                se.wait(v)
                ve.inc(v.tensor_scalar(ln_sb[:, 8 + t:9 + t], ln_sb[:, t:t + 1],
                                       1.0 / 1024.0, LN_EPS, ALU.mult, ALU.add))

            @block.scalar
            def _(s):
                ve.wait(s)
                se.inc(s.activation(ln_sb[:, 8 + t:9 + t], ln_sb[:, 8 + t:9 + t],
                                    AF.Sqrt))

            @block.vector
            def _(v):
                se.wait(v)
                ve.inc(v.reciprocal(ln_sb[:, 12 + t:13 + t], ln_sb[:, 8 + t:9 + t]))
                ve.wait(v)
                ve.inc(v.tensor_scalar_mul(xf_sb[:, t * 1024:(t + 1) * 1024],
                                           O_sb[:, t * 1024:(t + 1) * 1024],
                                           ln_sb[:, 12 + t:13 + t]))

            @block.sync
            def _(sy):
                ve.wait(sy)
                so.dma(sy.dma_start(
                    out=x3q[a * 512 + t * 128: a * 512 + (t + 1) * 128, :],
                    in_=xf_sb[:, t * 1024:(t + 1) * 1024]))

            @block.gpsimd
            def _(g):
                ve.wait(g)
                so.dma(g.dma_start(
                    out=x3bf[a * 512 + t * 128: a * 512 + (t + 1) * 128, :],
                    in_=xf_sb[:, t * 1024:(t + 1) * 1024]))

    return stB



def emit_l1_ffn(E, io):
    nc, block = E.nc, E.block
    ld, ls, pe, ve, se, so = E.ld, E.ls, E.pe, E.ve, E.se, E.so
    pS, pPV, pT0 = E.pS, E.pPV, E.pT0
    xq, x3bf = io["x3q_buf"], io["x3bf"]
    w1, w2 = io["f_w1"], io["f_w2"]
    b1_pc, b2_row = io["f_b1_pc"], io["f_b2_row"]
    eye = io["eye"]
    x4q = io["x4q"]
    bw1T_buf, bw2T_buf = io["bw1T_buf"], io["bw2T_buf"]

    stB = ExitStack()
    pT1 = stB.enter_context(nc.psum_tensor("b4_pT1", [128, 512], BF16))
    pTt = [pT0, pT1]
    eye_bf = stB.enter_context(nc.sbuf_tensor("b4_eye", [128, 128], BF16))
    ln_sb = stB.enter_context(nc.sbuf_tensor("b4_ln", [128, 16], F32))
    ones_r = stB.enter_context(nc.sbuf_tensor("b4_ones", [1, 128], F32))
    b1s = stB.enter_context(nc.sbuf_tensor("b4_b1s", [128, 32], F32))
    b2r = stB.enter_context(nc.sbuf_tensor("b4_b2r", [1, D], F32))
    b2b = stB.enter_context(nc.sbuf_tensor("b4_b2b", [128, D], F32))

    @block.sync
    def _(sy):
        ls.dma(sy.dma_start(out=b1s[:, :], in_=b1_pc))
        ls.dma(sy.dma_start(out=b2r[:, :], in_=b2_row))

    @block.gpsimd
    def _(g):
        ld.dma(g.dma_start(out=eye_bf[:, :], in_=eye))

    @block.vector
    def _(v):
        ve.inc(v.memset(ones_r[:, :], 1.0))

    @block.tensor
    def _(te):
        ls.wait(te)
        ve.wait(te)
        ld.wait(te)
        for h in range(2):
            pe.inc(te.matmul(pS[:, h * 512:(h + 1) * 512], ones_r[:, :],
                             b2r[:, h * 512:(h + 1) * 512], start=True, stop=True))
    pe_bc = pe.n

    @block.vector
    def _(v):
        v.wait_ge(pe.h, pe_bc)
        ve.inc(v.tensor_copy(b2b[:, :], pS[:, 0:1024]))

    st0 = ExitStack()
    wc_sb = st0.enter_context(nc.sbuf_tensor("b4_wc", [128, 4096], F32))
    sgn_sb = st0.enter_context(nc.sbuf_tensor("b4_sgn", [128, 4096], F32))
    bwc_sb = st0.enter_context(nc.sbuf_tensor("b4_bwc", [128, 4096], BF16))
    stage_sb = st0.enter_context(nc.sbuf_tensor("b4_stage", [128, 4096], BF16))
    al_sb = st0.enter_context(nc.sbuf_tensor("b4_al", [128, 1], F32))

    def emit_binarize(src, nrow_chunks, width, dst_buf):
        nw = width // 128
        for ch in range(nrow_chunks):
            @block.gpsimd
            def _(g):
                pe.wait(g)
                so.wait(g)
                ld.dma(g.dma_start(out=wc_sb[:, 0:width],
                                   in_=src[ch * 128:(ch + 1) * 128, :]))
            ld_now = ld.n

            @block.vector
            def _(v):
                v.wait_ge(ld.h, ld_now)
                ve.inc(v.tensor_reduce(al_sb[:, 0:1], wc_sb[:, 0:width], AX.X,
                                       ALU.add, apply_absolute_value=True))

            @block.scalar
            def _(s):
                ve.wait(s)
                se.inc(s.activation(sgn_sb[:, 0:width], wc_sb[:, 0:width], AF.Sign))

            @block.vector
            def _(v):
                se.wait(v)
                ve.inc(v.tensor_scalar(bwc_sb[:, 0:width], sgn_sb[:, 0:width],
                                       al_sb[:, 0:1], 1.0 / width,
                                       ALU.mult, ALU.mult))
            ve_bw = ve.n
            ve_cpb = ve.n
            pe_tb = pe.n

            @block.tensor
            def _(te):
                te.wait_ge(ve.h, ve_bw)
                for c in range(nw):
                    par = pTt[c % 2]
                    if c >= 2:
                        te.wait_ge(ve.h, ve_cpb + c - 1)
                    inst = te.transpose(par[:, 0:128],
                                        bwc_sb[:, c * 128:(c + 1) * 128],
                                        eye_bf[:, :])
                    pe.inc(inst)

            @block.vector
            def _(v):
                for c in range(nw):
                    v.wait_ge(pe.h, pe_tb + c + 1)
                    ve.inc(v.tensor_copy(stage_sb[:, c * 128:(c + 1) * 128],
                                         pTt[c % 2][:, 0:128]))
            ve_stg = ve.n

            @block.sync
            def _(sy):
                sy.wait_ge(ve.h, ve_stg)
                so.dma(sy.dma_start(
                    out=dst_buf.rearrange("(cc p) hh -> p cc hh", p=128)
                        [:, :, ch * 128:(ch + 1) * 128],
                    in_=stage_sb[:, 0:width].rearrange("p (cc q) -> p cc q", q=128)))

    emit_binarize(w1, 32, 1024, bw1T_buf)
    emit_binarize(w2, 8, 4096, bw2T_buf)
    st0.close()

    x_tm = stB.enter_context(nc.sbuf_tensor("b4_xtm", [128, 1024], BF16))
    xT = stB.enter_context(nc.sbuf_tensor("b4_xT", [128, 2 * 8 * 512], BF16))
    xf_sb = stB.enter_context(nc.sbuf_tensor("b4_xf", [128, 4 * 1024], F32))
    res_sb = stB.enter_context(nc.sbuf_tensor("b4_res", [128, 1024], F32))
    wbuf = stB.enter_context(nc.sbuf_tensor("b4_wbuf", [128, 32 * 1024], BF16))
    wbuf2 = stB.enter_context(nc.sbuf_tensor("b4_wbuf2", [128, 32 * 1024], BF16))
    hg_sb = stB.enter_context(nc.sbuf_tensor("b4_hg", [128, 32 * 512], BF16))

    xt_used = [pe.n, pe.n]

    def load_xT(grp, so_gate=False):
        slot = grp % 2
        base = slot * 4096
        guard = xt_used[slot]

        @block.sync
        def _(sy):
            sy.wait_ge(pe.h, guard)
            if so_gate:
                so.wait(sy)
            for c in range(8):
                ls.dma(sy.dma_start(
                    out=xT[:, base + c * 512: base + (c + 1) * 512],
                    in_=x3bf[grp * 512:(grp + 1) * 512, c * 128:(c + 1) * 128],
                    transpose=True))
        return base, ls.n

    nfo = load_xT(0, so_gate=True)
    for grp in range(NGRP):
        pe_wg = pe.n

        @block.sync
        def _(sy):
            sy.wait_ge(pe.h, pe_wg)
            so.wait(sy)
            for c in range(8):
                ls.dma(sy.dma_start(out=wbuf[:, c * 4096:(c + 1) * 4096],
                                    in_=bw1T_buf[c * 128:(c + 1) * 128, :]))
            for hc in range(32):
                ls.dma(sy.dma_start(out=wbuf2[:, hc * 1024:(hc + 1) * 1024],
                                    in_=bw2T_buf[hc * 128:(hc + 1) * 128, :]))
        ld_w1 = ls.n
        ld_w2 = ls.n

        xbase, xready = nfo
        if grp + 1 < NGRP:
            nfo = load_xT(grp + 1)

        @block.gpsimd
        def _(g):
            ve.wait(g)
            se.wait(g)
            so.wait(g)
            for t in range(4):
                ld.dma(g.dma_start(
                    out=xf_sb[:, t * 1024:(t + 1) * 1024],
                    in_=xq[grp * 512 + t * 128: grp * 512 + (t + 1) * 128, :]))
        ld_x = ld.n
        se_hb = se.n
        pe_hb = pe.n

        @block.tensor
        def _(te):
            te.wait_ge(ls.h, ld_w1)
            te.wait_ge(ls.h, xready)
            ve.wait(te)
            se.wait(te)
            for hc in range(32):
                if hc >= 4:
                    te.wait_ge(se.h, se_hb + hc - 3)
                for c in range(8):
                    inst = te.matmul(
                        pS[:, (hc % 4) * 512:(hc % 4 + 1) * 512],
                        wbuf[:, c * 4096 + hc * 128: c * 4096 + hc * 128 + 128],
                        xT[:, xbase + c * 512: xbase + (c + 1) * 512],
                        start=(c == 0), stop=(c == 7))
                pe.inc(inst)
        xt_used[grp % 2] = pe.n

        @block.scalar
        def _(s):
            for hc in range(32):
                s.wait_ge(pe.h, pe_hb + hc + 1)
                se.inc(s.activation(hg_sb[:, hc * 512:(hc + 1) * 512],
                                    pS[:, (hc % 4) * 512:(hc % 4 + 1) * 512],
                                    AF.Gelu, bias=b1s[:, hc:hc + 1]))

        se_hdone = se.n

        @block.vector
        def _(v):
            ve.wait(v)
            v.wait_ge(ld.h, ld_x)
            for t in range(4):
                ve.inc(v.tensor_tensor(xf_sb[:, t * 1024:(t + 1) * 1024],
                                       xf_sb[:, t * 1024:(t + 1) * 1024],
                                       b2b[:, :], ALU.add))

        evac_mark = {}
        for t in range(4):
            @block.tensor
            def _(te):
                te.wait_ge(ls.h, ld_w2)
                te.wait_ge(se.h, se_hdone)
                if t > 0:
                    te.wait_ge(ve.h, evac_mark[t - 1])
                for hc in range(32):
                    for h in range(2):
                        inst = te.matmul(
                            pPV[:, h * 512:(h + 1) * 512],
                            hg_sb[:, hc * 512 + t * 128: hc * 512 + t * 128 + 128],
                            wbuf2[:, hc * 1024 + h * 512: hc * 1024 + (h + 1) * 512],
                            start=(hc == 0), stop=(hc == 31))
                pe.inc(inst)
            pv_t = pe.n

            @block.vector
            def _(v):
                v.wait_ge(pe.h, pv_t)
                ve.inc(v.tensor_tensor(xf_sb[:, t * 1024:(t + 1) * 1024],
                                       xf_sb[:, t * 1024:(t + 1) * 1024],
                                       pPV[:, :], ALU.add))
            evac_mark[t] = ve.n

        for t in range(4):
            @block.vector
            def _(v):
                ve.wait(v)
                ve.inc(v.reduce_sum(ln_sb[:, t:t + 1],
                                    xf_sb[:, t * 1024:(t + 1) * 1024], axis=AX.X))
                ve.wait(v)
                ve.inc(v.tensor_scalar_mul(ln_sb[:, 4 + t:5 + t], ln_sb[:, t:t + 1],
                                           -1.0 / 1024.0))
                ve.wait(v)
                ve.inc(v.tensor_scalar_add(xf_sb[:, t * 1024:(t + 1) * 1024],
                                           xf_sb[:, t * 1024:(t + 1) * 1024],
                                           ln_sb[:, 4 + t:5 + t]))

            @block.scalar
            def _(s):
                ve.wait(s)
                se.inc(s.activation(x_tm[:, 0:1024],
                                    xf_sb[:, t * 1024:(t + 1) * 1024],
                                    AF.Square, accum_out=ln_sb[:, t:t + 1]))

            @block.vector
            def _(v):
                se.wait(v)
                ve.inc(v.tensor_scalar(ln_sb[:, 8 + t:9 + t], ln_sb[:, t:t + 1],
                                       1.0 / 1024.0, LN_EPS, ALU.mult, ALU.add))

            @block.scalar
            def _(s):
                ve.wait(s)
                se.inc(s.activation(ln_sb[:, 8 + t:9 + t], ln_sb[:, 8 + t:9 + t],
                                    AF.Sqrt))

            @block.vector
            def _(v):
                se.wait(v)
                ve.inc(v.reciprocal(ln_sb[:, 12 + t:13 + t], ln_sb[:, 8 + t:9 + t]))
                ve.wait(v)
                ve.inc(v.tensor_scalar_mul(xf_sb[:, t * 1024:(t + 1) * 1024],
                                           xf_sb[:, t * 1024:(t + 1) * 1024],
                                           ln_sb[:, 12 + t:13 + t]))

            @block.sync
            def _(sy):
                ve.wait(sy)
                so.dma(sy.dma_start(
                    out=x4q[grp * 512 + t * 128: grp * 512 + (t + 1) * 128, :],
                    in_=xf_sb[:, t * 1024:(t + 1) * 1024]))


    return stB



def build_fused(debug=False):
    nc = bass.Bass("TRN2", target_bir_lowering=False)
    io = {}

    def ein(name, shape, dt=F32):
        io[name] = nc.dram_tensor(name, shape, dt, kind="ExternalInput").ap()

    def internal(name, shape, dt):
        io[name] = nc.dram_tensor(name, shape, dt, kind="Internal").ap()

    ein("xb", [S, D]); ein("xq0", [TPC, D])
    ein("wqT", [D, D]); ein("wkT", [D, D]); ein("wvT", [D, D]); ein("woT", [D, D])
    ein("eye", [128, 128]); ein("dmask", [128, 512])
    ein("sgT", [D, 2048]); ein("suT", [D, 2048]); ein("sdT", [2048, D])
    ein("routerT", [D, 8]); ein("rbias", [1, 8])
    ein("wgT", [8, D, 2048]); ein("wuT", [8, D, 2048]); ein("wdT", [8, 2048, D])
    ein("onehot", [8, 8 * 128])
    ein("WqT", [D, D]); ein("WkT", [D, D]); ein("WvT", [D, D])
    ein("W1T", [D, D]); ein("W2T", [D, 4096]); ein("W3T", [4096, D])
    ein("bq_pc", [128, 8]); ein("bk_pc", [128, 8])
    ein("b1_pc", [128, 8]); ein("b2_pc", [128, 32])
    ein("bv_row", [1, D]); ein("b3_row", [1, D])
    ein("ecol", [8, 1])
    ein("f_w1", [4096, D]); ein("f_w2", [D, 4096])
    ein("f_b1_pc", [128, 32]); ein("f_b2_row", [1, D])
    io["x4q"] = nc.dram_tensor("x4q", [TPC, D], F32, kind="ExternalOutput").ap()
    internal("x1q_buf", [TPC, D], F32)
    internal("x2q_buf", [TPC, D], F32)
    internal("x3q_buf", [TPC, D], F32)
    internal("x2bf", [TPC, D], BF16)
    internal("xg", [NCORE * TPC, D], BF16)
    internal("xgb", [S, D], BF16)
    internal("kT_buf", [D, S], BF16)
    internal("v_buf", [S, D], BF16)
    internal("qT_buf", [D, TPC], BF16)
    internal("xb_bf", [S, D], BF16)
    internal("xq0bf", [TPC, D], BF16)
    internal("x1bf", [TPC, D], BF16)
    internal("x3bf", [TPC, D], BF16)
    internal("xe_all", [8 * CEXP, D], BF16)
    internal("ye_all", [8 * CEXP, D], BF16)
    internal("wbg", [9, D, 2048], BF16)
    internal("wbu", [9, D, 2048], BF16)
    internal("wbd", [9, 2048, D], BF16)
    internal("w1bf", [D, D], BF16)
    internal("w2bf", [D, 4096], BF16)
    internal("w3bf", [4096, D], BF16)
    internal("bw1T_buf", [D, 4096], BF16)
    internal("bw2T_buf", [4096, D], BF16)
    if debug:
        for k in ("dbg_x1", "dbg_x2", "dbg_x3"):
            io[k] = nc.dram_tensor(k, [TPC, D], F32, kind="ExternalOutput").ap()
        io["dbg_xgb"] = nc.dram_tensor("dbg_xgb", [S, D], BF16,
                                       kind="ExternalOutput").ap()
        io["dbg_kT"] = nc.dram_tensor("dbg_kT", [D, S], BF16,
                                      kind="ExternalOutput").ap()
        io["dbg_v"] = nc.dram_tensor("dbg_v", [S, D], BF16,
                                     kind="ExternalOutput").ap()
        io["dbg_qT"] = nc.dram_tensor("dbg_qT", [D, TPC], BF16,
                                      kind="ExternalOutput").ap()
        io["dbg_attn"] = nc.dram_tensor("dbg_attn", [TPC, D], BF16,
                                        kind="ExternalOutput").ap()
        io["dbg_stat"] = nc.dram_tensor("dbg_stat", [128, 24 * NGRP], F32,
                                        kind="ExternalOutput").ap()

    st = ExitStack()
    E = Env(nc, st)

    s1 = emit_l0_attn(E, io)
    emit_barrier(E)
    s1.close()
    s2 = emit_l0_ffn(E, io)
    emit_barrier(E)
    s2.close()
    s3 = emit_l1_attn(E, io)
    emit_barrier(E)
    s3.close()
    s4 = emit_l1_ffn(E, io)

    if debug:
        @E.block.gpsimd
        def _(g):
            for s in (E.ld, E.ls, E.pe, E.ve, E.se, E.so):
                s.wait(g)
            E.so.dma(g.dma_start(out=io["dbg_x1"], in_=io["x1q_buf"]))
            E.so.dma(g.dma_start(out=io["dbg_x2"], in_=io["x2q_buf"]))
            E.so.dma(g.dma_start(out=io["dbg_x3"], in_=io["x3q_buf"]))
            E.so.dma(g.dma_start(out=io["dbg_xgb"], in_=io["xgb"]))
            E.so.dma(g.dma_start(out=io["dbg_kT"], in_=io["kT_buf"]))
            E.so.dma(g.dma_start(out=io["dbg_v"], in_=io["v_buf"]))
            E.so.dma(g.dma_start(out=io["dbg_qT"], in_=io["qT_buf"]))

    @E.block.sync
    def _(sy):
        E.so.wait(sy)

    s4.close()
    st.close()
    return nc



_EYE = np.eye(128, dtype=np.float32)


def _make_onehot():
    oh = np.zeros((8, 8 * 128), dtype=np.float32)
    for e in range(8):
        oh[e, e * 128:(e + 1) * 128] = 1.0
    return oh


def _pc(b):
    b = np.asarray(b)
    return np.ascontiguousarray(b.reshape(-1, 128).T)


def _t(a):
    return np.ascontiguousarray(np.asarray(a).T)


def stage_fused(inputs):
    x = np.asarray(inputs["L0_attn_in_features"])
    shared = {
        "wqT": _t(inputs["L0_attn_w_q"]), "wkT": _t(inputs["L0_attn_w_k"]),
        "wvT": _t(inputs["L0_attn_w_v"]), "woT": _t(inputs["L0_attn_w_o"]),
        "eye": _EYE,
        "sgT": _t(inputs["L0_ffn_sg_w"]), "suT": _t(inputs["L0_ffn_su_w"]),
        "sdT": _t(inputs["L0_ffn_sd_w"]),
        "routerT": _t(inputs["L0_ffn_router_w"]),
        "rbias": np.asarray(inputs["L0_ffn_routing_bias"]).reshape(1, 8),
        "wgT": np.ascontiguousarray(np.swapaxes(np.asarray(inputs["L0_ffn_wg"]), 1, 2)),
        "wuT": np.ascontiguousarray(np.swapaxes(np.asarray(inputs["L0_ffn_wu"]), 1, 2)),
        "wdT": np.ascontiguousarray(np.swapaxes(np.asarray(inputs["L0_ffn_wd"]), 1, 2)),
        "onehot": _make_onehot(),
        "WqT": _t(inputs["L1_attn_Wq"]), "WkT": _t(inputs["L1_attn_Wk"]),
        "WvT": _t(inputs["L1_attn_Wv"]),
        "W1T": _t(inputs["L1_attn_W1"]), "W2T": _t(inputs["L1_attn_W2"]),
        "W3T": _t(inputs["L1_attn_W3"]),
        "bq_pc": _pc(inputs["L1_attn_bq"]), "bk_pc": _pc(inputs["L1_attn_bk"]),
        "b1_pc": _pc(inputs["L1_attn_b1"]), "b2_pc": _pc(inputs["L1_attn_b2"]),
        "bv_row": np.asarray(inputs["L1_attn_bv"]).reshape(1, D).astype(np.float32),
        "b3_row": np.asarray(inputs["L1_attn_b3"]).reshape(1, D).astype(np.float32),
        "ecol": (np.arange(8, dtype=np.float32) * CEXP).reshape(8, 1),
        "f_w1": np.ascontiguousarray(np.asarray(inputs["L1_ffn_w1"])),
        "f_w2": np.ascontiguousarray(np.asarray(inputs["L1_ffn_w2"])),
        "f_b1_pc": _pc(inputs["L1_ffn_b1"]),
        "f_b2_row": np.asarray(inputs["L1_ffn_b2"]).reshape(1, D).astype(np.float32),
    }
    maps = []
    for c in range(NCORE):
        b, g = c // GPB, c % GPB
        rows = own_token_idx(g)
        m = dict(shared)
        m["xb"] = np.ascontiguousarray(x[b])
        m["xq0"] = np.ascontiguousarray(x[b][rows])
        m["dmask"] = make_dmask(g)
        maps.append(m)
    return maps


def kernel(**inputs):
    nc = build_fused()
    maps = stage_fused(inputs)
    res = run_bass_kernel_spmd(nc, maps, core_ids=list(range(NCORE)))
    out = np.empty((B, S, D), dtype=np.float32)
    for c in range(NCORE):
        b, g = c // GPB, c % GPB
        out[b, own_token_idx(g)] = res.results[c]["x4q"]
    return out


# version 13 (final)
# speedup vs baseline: 1.0628x; 1.0628x over previous
import numpy as np
from contextlib import ExitStack

import concourse.bass as bass
import concourse.mybir as mybir
from concourse.bass_utils import run_bass_kernel_spmd

F32 = mybir.dt.float32
BF16 = mybir.dt.bfloat16
I32 = mybir.dt.int32
AF = mybir.ActivationFunctionType
ALU = mybir.AluOpType
AX = mybir.AxisListType

B = 2
S = 16384
D = 1024
NCORE = 8
GPB = 4
TPC = S // GPB
NQT = TPC // 128
NGRP = NQT // 4
NEG = -1e30
LN_EPS = 1e-5
CEXP = 384
NTE = CEXP // 128


def own_token_idx(g):
    idx = []
    for i in range(NQT):
        qt = 4 * i + g
        idx.append(np.arange(qt * 128, qt * 128 + 128))
    return np.concatenate(idx)


def make_dmask(g):
    ql = np.arange(128)[:, None]
    j = np.arange(512)[None, :]
    return np.where(j <= 128 * g + ql, 0.0, NEG).astype(np.float32)


class Sem:

    def __init__(self, stack, nc, name):
        self.h = stack.enter_context(nc.semaphore(name))
        self.n = 0

    def inc(self, inst, k=1):
        inst.then_inc(self.h, k)
        self.n += k
        return self.n

    def dma(self, inst):
        return self.inc(inst, 16)

    def wait(self, eng, n=None):
        eng.wait_ge(self.h, self.n if n is None else n)


class Env:

    def __init__(self, nc, stack):
        self.nc = nc
        self.block = stack.enter_context(nc.Block())
        self.ld = Sem(stack, nc, "ld")
        self.ls = Sem(stack, nc, "ls")
        self.pe = Sem(stack, nc, "pe")
        self.ve = Sem(stack, nc, "ve")
        self.se = Sem(stack, nc, "se")
        self.so = Sem(stack, nc, "so")
        self.cc = Sem(stack, nc, "cc")
        self.cg = Sem(stack, nc, "cg")
        self.pS = stack.enter_context(nc.psum_tensor("pS", [128, 2048], F32))
        self.pPV = stack.enter_context(nc.psum_tensor("pPV", [128, 1024], F32))
        self.pT0 = stack.enter_context(nc.psum_tensor("pT0", [128, 512], BF16))


def emit_barrier(E):
    block = E.block
    sems = [E.ld, E.ls, E.pe, E.ve, E.se, E.so]
    for sec in (block.tensor, block.vector, block.scalar, block.sync,
                block.gpsimd):
        @sec
        def _(e):
            for s in sems:
                s.wait(e)




def emit_flash_group(E, a, P, st8):
    nc, block = E.nc, E.block
    ld, ls, pe, ve, se, so = E.ld, E.ls, E.pe, E.ve, E.se, E.so
    pS, pPV = E.pS, E.pPV
    qT_sb, kTm_sb, vm_sb = P["qT_sb"], P["kTm_sb"], P["vm_sb"]
    p_sb, pT_sb, O_sb, stat = P["p_sb"], P["pT_sb"], P["O_sb"], P["stat"]
    dm_sb, eye_bf, pTt = P["dm_sb"], P["eye_bf"], P["pTt"]
    kT_buf, v_buf = P["kT_buf"], P["v_buf"]
    SC = P["SC"]
    causal = P["causal"]
    nmac = P["nmac"]

    def n512(t):
        return (4 * a + t + 1) if causal else 32

    def nch_of(t, kb):
        return max(0, min(2, n512(t) - 2 * kb))

    def emit_kv_load(kb):
        slot = kb % 2
        guard = st8["kvfree"][slot]

        @block.sync
        def _(sy):
            sy.wait_ge(pe.h, guard)
            ls.dma(sy.dma_start(
                out=kTm_sb[:, slot * 8192:(slot + 1) * 8192]
                    .rearrange("p (c t) -> p c t", c=8),
                in_=kT_buf[:, kb * 1024:(kb + 1) * 1024]
                    .rearrange("(c p) t -> p c t", p=128)))
            ls.dma(sy.dma_start(
                out=vm_sb[:, slot * 8192:(slot + 1) * 8192]
                    .rearrange("p (a d) -> p a d", a=8),
                in_=v_buf[kb * 1024:(kb + 1) * 1024, :]
                    .rearrange("(a p) d -> p a d", p=128)))
        st8["kvready"][slot] = ls.n

    emit_kv_load(0)
    for kb in range(nmac):
        kslot = kb % 2
        kv_ready = st8["kvready"][kslot]
        kbase = kslot * 8192
        if kb + 1 < nmac:
            emit_kv_load(kb + 1)
        first = (kb == 0)
        seq = [t for t in range(4) if nch_of(t, kb) > 0]

        def emit_S(t):
            nch = nch_of(t, kb)
            sl = t % 2
            sbase = sl * 1024
            w_exp = st8["exp"][sl]
            w_sts = st8["sts"][sl]

            @block.tensor
            def _(te):
                te.wait_ge(ls.h, kv_ready)
                te.wait_ge(se.h, w_exp)
                te.wait_ge(ve.h, w_sts)
                for ch in range(nch):
                    for c in range(8):
                        inst = te.matmul(
                            pS[:, sbase + ch * 512: sbase + (ch + 1) * 512],
                            qT_sb[:, c * 512 + t * 128: c * 512 + t * 128 + 128],
                            kTm_sb[:, kbase + c * 1024 + ch * 512:
                                   kbase + c * 1024 + (ch + 1) * 512],
                            start=(c == 0), stop=(c == 7))
                pe.inc(inst)
            return pe.n

        def emit_softmax(t, s_done):
            nch = nch_of(t, kb)
            sl = t % 2
            sbase = sl * 1024
            ncol = nch * 512

            @block.vector
            def _(v):
                v.wait_ge(pe.h, s_done)
                ve.wait(v, st8["oup"])
                if causal and (4 * a + t) // 2 == kb:
                    lc = (4 * a + t) % 2
                    ve.inc(v.tensor_tensor(
                        pS[:, sbase + lc * 512: sbase + (lc + 1) * 512],
                        pS[:, sbase + lc * 512: sbase + (lc + 1) * 512],
                        dm_sb[:, :], ALU.add))
                    ve.wait(v)
                ve.inc(v.reduce_max(stat[:, 8 + t:9 + t],
                                    pS[:, sbase: sbase + ncol], axis=AX.X))
                ve.wait(v)
                if first:
                    ve.inc(v.tensor_copy(stat[:, t:t + 1], stat[:, 8 + t:9 + t]))
                else:
                    ve.inc(v.tensor_max(stat[:, 12 + t:13 + t], stat[:, t:t + 1],
                                        stat[:, 8 + t:9 + t]))
                    ve.wait(v)
                    ve.inc(v.tensor_sub(stat[:, 16 + t:17 + t], stat[:, t:t + 1],
                                        stat[:, 12 + t:13 + t]))
                    ve.wait(v)
                    ve.inc(v.tensor_copy(stat[:, t:t + 1], stat[:, 12 + t:13 + t]))
                ve.wait(v)
                ve.inc(v.tensor_scalar_mul(stat[:, 20 + t:21 + t],
                                           stat[:, t:t + 1], -SC))
            st8["sts"][sl] = ve.n

            @block.scalar
            def _(s):
                ve.wait(s)
                s.wait_ge(pe.h, st8["ptdone"][sl])
                if not first:
                    se.inc(s.activation(stat[:, 16 + t:17 + t],
                                        stat[:, 16 + t:17 + t],
                                        AF.Exp, bias=0.0, scale=SC))
                se.inc(s.activation(p_sb[:, sbase: sbase + ncol],
                                    pS[:, sbase: sbase + ncol], AF.Exp,
                                    bias=stat[:, 20 + t:21 + t], scale=SC,
                                    accum_out=stat[:, 8 + t:9 + t]))
            st8["exp"][sl] = se.n

            @block.vector
            def _(v):
                se.wait(v)
                if first:
                    ve.inc(v.tensor_copy(stat[:, 4 + t:5 + t], stat[:, 8 + t:9 + t]))
                else:
                    ve.inc(v.scalar_tensor_tensor(stat[:, 4 + t:5 + t],
                                                  stat[:, 4 + t:5 + t],
                                                  stat[:, 16 + t:17 + t],
                                                  stat[:, 8 + t:9 + t],
                                                  ALU.mult, ALU.add))
            st8["oup"] = ve.n

        def emit_ptpv(t):
            nch = nch_of(t, kb)
            sl = t % 2
            sbase = sl * 1024
            ve_ptb = ve.n
            pe_ptb = pe.n

            @block.tensor
            def _(te):
                te.wait_ge(se.h, st8["exp"][sl])
                te.wait_ge(ve.h, ve_ptb)
                for ch in range(nch):
                    par = pTt[ch % 2]
                    for sf in range(4):
                        inst = te.transpose(
                            par[:, sf * 128:(sf + 1) * 128],
                            p_sb[:, sbase + ch * 512 + sf * 128:
                                 sbase + ch * 512 + (sf + 1) * 128],
                            eye_bf[:, :])
                    pe.inc(inst)
            st8["ptdone"][sl] = pe.n

            @block.vector
            def _(v):
                v.wait_ge(pe.h, st8["pvdone"][sl])
                for ch in range(nch):
                    v.wait_ge(pe.h, pe_ptb + ch + 1)
                    ve.inc(v.tensor_copy(
                        pT_sb[:, sbase + ch * 512: sbase + (ch + 1) * 512],
                        pTt[ch % 2][:, :]))
            cp_done = ve.n

            @block.tensor
            def _(te):
                te.wait_ge(ve.h, cp_done)
                te.wait_ge(ve.h, st8["odone"])
                nk = nch * 4
                for ks in range(nk):
                    for h in range(2):
                        inst = te.matmul(
                            pPV[:, h * 512:(h + 1) * 512],
                            pT_sb[:, sbase + ks * 128: sbase + (ks + 1) * 128],
                            vm_sb[:, kbase + ks * 1024 + h * 512:
                                  kbase + ks * 1024 + (h + 1) * 512],
                            start=(ks == 0), stop=(ks == nk - 1))
                pe.inc(inst)
            st8["pvdone"][sl] = pe.n
            pv_done = pe.n

            @block.vector
            def _(v):
                v.wait_ge(pe.h, pv_done)
                if first:
                    ve.inc(v.tensor_copy(O_sb[:, t * 1024:(t + 1) * 1024],
                                         pPV[:, :]))
                else:
                    ve.inc(v.scalar_tensor_tensor(
                        O_sb[:, t * 1024:(t + 1) * 1024],
                        O_sb[:, t * 1024:(t + 1) * 1024],
                        stat[:, 16 + t:17 + t], pPV[:, :], ALU.mult, ALU.add))
            st8["odone"] = ve.n

        s_done = {}
        s_done[seq[0]] = emit_S(seq[0])
        for idx, t in enumerate(seq):
            emit_softmax(t, s_done[t])
            if idx + 1 < len(seq):
                s_done[seq[idx + 1]] = emit_S(seq[idx + 1])
            emit_ptpv(t)
        st8["kvfree"][kslot] = pe.n




def precast_plan(io):
    L = []
    for e in [8] + list(range(8)):
        if e == 8:
            L += [(io["wbg"][8], io["sgT"], None), (io["wbu"][8], io["suT"], None),
                  (io["wbd"][8], io["sdT"], 8)]
        else:
            L += [(io["wbg"][e], io["wgT"][e], None),
                  (io["wbu"][e], io["wuT"][e], None),
                  (io["wbd"][e], io["wdT"][e], e)]
    L += [(io["w1bf"], io["W1T"], None), (io["w2bf"], io["W2T"], None),
          (io["w3bf"], io["W3T"], "mlp")]
    return L


def emit_precast_chunk(E, plan, lo, hi):
    block, ld, pe = E.block, E.ld, E.pe
    chunk = plan[lo:hi]
    if not chunk:
        return
    gate = pe.n

    @block.gpsimd
    def _(g):
        g.wait_ge(pe.h, gate)
        for dst, srcap, key in chunk:
            ld.dma(g.dma_start(out=dst, in_=srcap))
            if key == "mlp":
                E.mlp_cast = ld.n
            elif key is not None:
                E.cast_done[key] = ld.n


def emit_precast_spread(E, items, rows, pe_lo, pe_hi):
    block, ld, pe = E.block, E.ld, E.pe
    subs = []
    for (dst, srcap, key), nr in zip(items, rows):
        h = nr // 2
        subs.append((dst[0:h, :], srcap[0:h, :], None))
        subs.append((dst[h:nr, :], srcap[h:nr, :], key))
    n = len(subs)
    if n == 0:
        return
    for j, (d, s, key) in enumerate(subs):
        gate = pe_lo + ((pe_hi - pe_lo) * j) // n

        @block.gpsimd
        def _(g):
            g.wait_ge(pe.h, gate)
            ld.dma(g.dma_start(out=d, in_=s))
            if key == "mlp":
                E.mlp_cast = ld.n
            elif key is not None:
                E.cast_done[key] = ld.n



def emit_l0_attn(E, io):
    nc, block = E.nc, E.block
    ld, ls, pe, ve, se, so = E.ld, E.ls, E.pe, E.ve, E.se, E.so
    pS, pPV, pT0 = E.pS, E.pPV, E.pT0
    xb, xq = io["xb"], io["xq0"]
    wqT, wkT, wvT, woT = io["wqT"], io["wkT"], io["wvT"], io["woT"]
    eye, dmask = io["eye"], io["dmask"]
    x1q = io["x1q_buf"]
    kT_buf, v_buf, qT_buf = io["kT_buf"], io["v_buf"], io["qT_buf"]

    stB = ExitStack()
    pT1 = stB.enter_context(nc.psum_tensor("b1_pT1", [128, 512], BF16))
    pTt = [pT0, pT1]
    wo_sb = stB.enter_context(nc.sbuf_tensor("b1_wo", [128, 8 * 1024], BF16))
    eye_bf = stB.enter_context(nc.sbuf_tensor("b1_eye", [128, 128], BF16))
    dm_sb = stB.enter_context(nc.sbuf_tensor("b1_dm", [128, 512], F32))
    stat = stB.enter_context(nc.sbuf_tensor("b1_stat", [128, 24], F32))
    ln_sb = stB.enter_context(nc.sbuf_tensor("b1_ln", [128, 16], F32))

    stA = ExitStack()
    wq_sb = stA.enter_context(nc.sbuf_tensor("b1_wq", [128, 8 * 1024], BF16))
    wk_sb = stA.enter_context(nc.sbuf_tensor("b1_wk", [128, 8 * 1024], BF16))
    wv_sb = stA.enter_context(nc.sbuf_tensor("b1_wv", [128, 8 * 1024], BF16))
    x_tm = stA.enter_context(nc.sbuf_tensor("b1_xtm", [128, 4 * 1024], BF16))
    xT = stA.enter_context(nc.sbuf_tensor("b1_xT", [128, 8 * 512], BF16))
    kv_stage = stA.enter_context(nc.sbuf_tensor("b1_kvst", [128, 8 * 512], BF16))
    v_stage = stA.enter_context(nc.sbuf_tensor("b1_vst", [128, 4 * 1024], BF16))

    @block.gpsimd
    def _(g):
        for w_sb, src in ((wq_sb, wqT), (wk_sb, wkT), (wv_sb, wvT), (wo_sb, woT)):
            for c in range(8):
                ld.dma(g.dma_start(out=w_sb[:, c * 1024:(c + 1) * 1024],
                                   in_=src[c * 128:(c + 1) * 128, :]))
        ld.dma(g.dma_start(out=eye_bf[:, :], in_=eye))

    @block.sync
    def _(sy):
        ls.dma(sy.dma_start(out=dm_sb[:, :], in_=dmask))

    def emit_group_xT(src, grp, pe_guard):
        @block.gpsimd
        def _(g):
            g.wait_ge(pe.h, pe_guard)
            for t in range(4):
                ld.dma(g.dma_start(
                    out=x_tm[:, t * 1024:(t + 1) * 1024],
                    in_=src[grp * 512 + t * 128: grp * 512 + (t + 1) * 128, :]))
        ld_now = ld.n
        ve_base = ve.n
        pe_base = pe.n

        @block.tensor
        def _(te):
            te.wait_ge(ld.h, ld_now)
            te.wait_ge(ve.h, ve_base)
            for c in range(8):
                par = pTt[c % 2]
                if c >= 2:
                    te.wait_ge(ve.h, ve_base + c - 1)
                for t in range(4):
                    inst = te.transpose(
                        par[:, t * 128:(t + 1) * 128],
                        x_tm[:, t * 1024 + c * 128: t * 1024 + (c + 1) * 128],
                        eye_bf[:, :])
                pe.inc(inst)

        @block.vector
        def _(v):
            for c in range(8):
                v.wait_ge(pe.h, pe_base + c + 1)
                ve.inc(v.tensor_copy(xT[:, c * 512:(c + 1) * 512], pTt[c % 2][:, :]))
        return pe.n

    def emit_proj_featT(w_sb, dst_stage):
        ve_now = ve.n
        se_base = se.n
        pe_base = pe.n

        @block.tensor
        def _(te):
            te.wait_ge(ve.h, ve_now)
            te.wait_ge(se.h, se_base)
            for oc in range(8):
                if oc >= 4:
                    te.wait_ge(se.h, se_base + oc - 3)
                for c in range(8):
                    inst = te.matmul(
                        pS[:, (oc % 4) * 512:(oc % 4 + 1) * 512],
                        w_sb[:, c * 1024 + oc * 128: c * 1024 + oc * 128 + 128],
                        xT[:, c * 512:(c + 1) * 512],
                        start=(c == 0), stop=(c == 7))
                pe.inc(inst)

        @block.scalar
        def _(s):
            for oc in range(8):
                s.wait_ge(pe.h, pe_base + oc + 1)
                se.inc(s.activation(dst_stage[:, oc * 512:(oc + 1) * 512],
                                    pS[:, (oc % 4) * 512:(oc % 4 + 1) * 512],
                                    AF.Copy))

    tr_guard = 0
    for grp in range(32):
        tr_guard = emit_group_xT(xb, grp, tr_guard)
        emit_proj_featT(wk_sb, kv_stage)
        se_k = se.n

        @block.sync
        def _(sy):
            sy.wait_ge(se.h, se_k)
            for oc in range(8):
                so.dma(sy.dma_start(
                    out=kT_buf[oc * 128:(oc + 1) * 128, grp * 512:(grp + 1) * 512],
                    in_=kv_stage[:, oc * 512:(oc + 1) * 512]))

        ve_vbase = ve.n
        pe_vbase = pe.n
        for t in range(4):
            @block.tensor
            def _(te):
                if t == 0:
                    te.wait_ge(ve.h, ve_vbase)
                else:
                    te.wait_ge(ve.h, ve_vbase + t)
                for c in range(8):
                    for h in range(2):
                        inst = te.matmul(
                            pPV[:, h * 512:(h + 1) * 512],
                            xT[:, c * 512 + t * 128: c * 512 + t * 128 + 128],
                            wv_sb[:, c * 1024 + h * 512: c * 1024 + (h + 1) * 512],
                            start=(c == 0), stop=(c == 7))
                pe.inc(inst)

            @block.vector
            def _(v):
                v.wait_ge(pe.h, pe_vbase + t + 1)
                ve.inc(v.tensor_copy(v_stage[:, t * 1024:(t + 1) * 1024], pPV[:, :]))

        ve_v = ve.n

        @block.sync
        def _(sy):
            sy.wait_ge(ve.h, ve_v)
            for t in range(4):
                so.dma(sy.dma_start(
                    out=v_buf[grp * 512 + t * 128: grp * 512 + (t + 1) * 128, :],
                    in_=v_stage[:, t * 1024:(t + 1) * 1024]))

        @block.tensor
        def _(te):
            so.wait(te)

    for grp in range(8):
        tr_guard = emit_group_xT(xq, grp, tr_guard)
        emit_proj_featT(wq_sb, kv_stage)
        se_q = se.n

        @block.sync
        def _(sy):
            sy.wait_ge(se.h, se_q)
            for oc in range(8):
                so.dma(sy.dma_start(
                    out=qT_buf[oc * 128:(oc + 1) * 128, grp * 512:(grp + 1) * 512],
                    in_=kv_stage[:, oc * 512:(oc + 1) * 512]))

        @block.tensor
        def _(te):
            so.wait(te)

    cast_plan = precast_plan(io)
    cast_rows = [1024, 1024, 2048] * 9 + [1024, 1024, 4096]
    E.cast_done = {}
    E.mlp_cast = None
    cast_bounds = []
    _cum = 0
    for _a in range(NGRP):
        _cum += 2 * _a + 2
        cast_bounds.append(round(len(cast_plan) * _cum / 72))

    stA.close()
    qT_sb = stB.enter_context(nc.sbuf_tensor("b1_qT", [128, 8 * 512], BF16))
    kTm_sb = stB.enter_context(nc.sbuf_tensor("b1_kTm", [128, 8 * 2048], BF16))
    vm_sb = stB.enter_context(nc.sbuf_tensor("b1_vm", [128, 16 * 1024], BF16))
    p_sb = stB.enter_context(nc.sbuf_tensor("b1_p", [128, 2048], BF16))
    pT_sb = stB.enter_context(nc.sbuf_tensor("b1_pT", [128, 16 * 128], BF16))
    O_sb = stB.enter_context(nc.sbuf_tensor("b1_O", [128, 4 * 1024], F32))
    Obf_sb = stB.enter_context(nc.sbuf_tensor("b1_Obf", [128, 4 * 1024], BF16))
    oT_sb = stB.enter_context(nc.sbuf_tensor("b1_oT", [128, 8 * 512], BF16))
    xf_sb = stB.enter_context(nc.sbuf_tensor("b1_xf", [128, 4 * 1024], F32))
    res_sb = stB.enter_context(nc.sbuf_tensor("b1_res", [128, 4 * 1024], F32))
    sq_sb = stB.enter_context(nc.sbuf_tensor("b1_sq", [128, 1024], F32))
    xo_sb = stB.enter_context(nc.sbuf_tensor("b1_xo", [128, 4 * 1024], F32))

    st8 = dict(exp=[se.n, se.n], sts=[ve.n, ve.n], ptdone=[pe.n, pe.n],
               pvdone=[pe.n, pe.n], oup=ve.n, odone=ve.n, kvfree=[pe.n, pe.n],
               kvready=[ls.n, ls.n])
    for a in range(NGRP):
        @block.sync
        def _(sy):
            pe.wait(sy)
            so.wait(sy)
            for c in range(8):
                ls.dma(sy.dma_start(
                    out=qT_sb[:, c * 512:(c + 1) * 512],
                    in_=qT_buf[c * 128:(c + 1) * 128, a * 512:(a + 1) * 512]))

        @block.sync
        def _(sy):
            ve.wait(sy)
            for t in range(4):
                ls.dma(sy.dma_start(
                    out=xf_sb[:, t * 1024:(t + 1) * 1024],
                    in_=xq[a * 512 + t * 128: a * 512 + (t + 1) * 128, :]))

        pe_pre = pe.n
        emit_flash_group(E, a, dict(
            causal=True, SC=1.0 / 32.0, nmac=2 * a + 2,
            qT_sb=qT_sb, kTm_sb=kTm_sb, vm_sb=vm_sb, p_sb=p_sb, pT_sb=pT_sb,
            O_sb=O_sb, stat=stat, dm_sb=dm_sb, eye_bf=eye_bf, pTt=pTt,
            kT_buf=kT_buf, v_buf=v_buf), st8)
        lo_i = cast_bounds[a - 1] if a else 0
        emit_precast_spread(E, cast_plan[lo_i:cast_bounds[a]],
                            cast_rows[lo_i:cast_bounds[a]], pe_pre, pe.n)

        @block.vector
        def _(v):
            ve.wait(v)
            for t in range(4):
                ve.inc(v.reciprocal(stat[:, 12 + t:13 + t], stat[:, 4 + t:5 + t]))
            ve.wait(v)
            for t in range(4):
                ve.inc(v.tensor_scalar_mul(Obf_sb[:, t * 1024:(t + 1) * 1024],
                                           O_sb[:, t * 1024:(t + 1) * 1024],
                                           stat[:, 12 + t:13 + t]))

        ve_otb = ve.n
        pe_otb = pe.n

        @block.tensor
        def _(te):
            te.wait_ge(ve.h, ve_otb)
            for c in range(8):
                par = pTt[c % 2]
                if c >= 2:
                    te.wait_ge(ve.h, ve_otb + c - 1)
                for t in range(4):
                    inst = te.transpose(
                        par[:, t * 128:(t + 1) * 128],
                        Obf_sb[:, t * 1024 + c * 128: t * 1024 + (c + 1) * 128],
                        eye_bf[:, :])
                pe.inc(inst)

        @block.vector
        def _(v):
            for c in range(8):
                v.wait_ge(pe.h, pe_otb + c + 1)
                ve.inc(v.tensor_copy(oT_sb[:, c * 512:(c + 1) * 512],
                                     pTt[c % 2][:, :]))

        ld_res = ls.n
        ev_mark = {}
        for t in range(4):
            @block.tensor
            def _(te):
                if t == 0:
                    ve.wait(te)
                else:
                    te.wait_ge(ve.h, ev_mark[t - 1])
                for c in range(8):
                    for h in range(2):
                        inst = te.matmul(
                            pPV[:, h * 512:(h + 1) * 512],
                            oT_sb[:, c * 512 + t * 128: c * 512 + t * 128 + 128],
                            wo_sb[:, c * 1024 + h * 512: c * 1024 + (h + 1) * 512],
                            start=(c == 0), stop=(c == 7))
                pe.inc(inst)
            pv_t = pe.n

            @block.vector
            def _(v):
                v.wait_ge(pe.h, pv_t)
                v.wait_ge(ls.h, ld_res)
                ve.inc(v.tensor_tensor(res_sb[:, t * 1024:(t + 1) * 1024], pPV[:, :],
                                       xf_sb[:, t * 1024:(t + 1) * 1024], ALU.add))
            ev_mark[t] = ve.n

        for t in range(4):
            @block.vector
            def _(v):
                ve.wait(v)
                ve.inc(v.reduce_sum(ln_sb[:, t:t + 1],
                                    res_sb[:, t * 1024:(t + 1) * 1024], axis=AX.X))
                ve.wait(v)
                ve.inc(v.tensor_scalar_mul(ln_sb[:, 4 + t:5 + t], ln_sb[:, t:t + 1],
                                           -1.0 / 1024.0))
                ve.wait(v)
                ve.inc(v.tensor_scalar_add(res_sb[:, t * 1024:(t + 1) * 1024],
                                           res_sb[:, t * 1024:(t + 1) * 1024],
                                           ln_sb[:, 4 + t:5 + t]))

            @block.scalar
            def _(s):
                ve.wait(s)
                se.inc(s.activation(sq_sb[:, :], res_sb[:, t * 1024:(t + 1) * 1024],
                                    AF.Square, accum_out=ln_sb[:, t:t + 1]))

            @block.vector
            def _(v):
                se.wait(v)
                ve.inc(v.tensor_scalar(ln_sb[:, 8 + t:9 + t], ln_sb[:, t:t + 1],
                                       1.0 / 1024.0, LN_EPS, ALU.mult, ALU.add))

            @block.scalar
            def _(s):
                ve.wait(s)
                se.inc(s.activation(ln_sb[:, 8 + t:9 + t], ln_sb[:, 8 + t:9 + t],
                                    AF.Sqrt))

            @block.vector
            def _(v):
                se.wait(v)
                ve.inc(v.reciprocal(ln_sb[:, 12 + t:13 + t], ln_sb[:, 8 + t:9 + t]))
                ve.wait(v)
                ve.inc(v.tensor_scalar_mul(xo_sb[:, t * 1024:(t + 1) * 1024],
                                           res_sb[:, t * 1024:(t + 1) * 1024],
                                           ln_sb[:, 12 + t:13 + t]))

            @block.sync
            def _(sy):
                ve.wait(sy)
                so.dma(sy.dma_start(
                    out=x1q[a * 512 + t * 128: a * 512 + (t + 1) * 128, :],
                    in_=xo_sb[:, t * 1024:(t + 1) * 1024]))

    return stB



def emit_l0_ffn(E, io):
    nc, block = E.nc, E.block
    ld, ls, pe, ve, se, so, cc = E.ld, E.ls, E.pe, E.ve, E.se, E.so, E.cc
    pGU, pY, pT0 = E.pS, E.pPV, E.pT0
    xq = io["x1q_buf"]
    sgT, suT, sdT = io["sgT"], io["suT"], io["sdT"]
    routerT, rbias = io["routerT"], io["rbias"]
    wgT, wuT, wdT = io["wgT"], io["wuT"], io["wdT"]
    eye, onehot = io["eye"], io["onehot"]
    x2q, x2bf, xg = io["x2q_buf"], io["x2bf"], io["xg"]
    wbg, wbu, wbd = io["wbg"], io["wbu"], io["wbd"]
    xe_all, ye_all = io["xe_all"], io["ye_all"]

    stB = ExitStack()
    pB = stB.enter_context(nc.psum_tensor("b2_pB", [128, 512], F32))
    eye_bf = stB.enter_context(nc.sbuf_tensor("b2_eyebf", [128, 128], BF16))
    eye_f = stB.enter_context(nc.sbuf_tensor("b2_eyef", [128, 128], F32))
    rT_sb = stB.enter_context(nc.sbuf_tensor("b2_rT", [128, 8 * 8], F32))
    oh_sb = stB.enter_context(nc.sbuf_tensor("b2_oh", [8, 8 * 128], F32))
    rb_sb = stB.enter_context(nc.sbuf_tensor("b2_rb", [1, 8], F32))
    rbbc_sb = stB.enter_context(nc.sbuf_tensor("b2_rbbc", [128, 8], F32))
    x_tm = stB.enter_context(nc.sbuf_tensor("b2_xtm", [128, 4 * 1024], BF16))
    xT = stB.enter_context(nc.sbuf_tensor("b2_xT", [128, 8 * 512], BF16))
    xf_sb = stB.enter_context(nc.sbuf_tensor("b2_xf", [128, 4 * 1024], F32))
    xc_sb = stB.enter_context(nc.sbuf_tensor("b2_xc", [128, 512], F32))
    wbuf = stB.enter_context(nc.sbuf_tensor("b2_wbuf", [128, 2 * 16384], BF16))
    wd_sb = stB.enter_context(nc.sbuf_tensor("b2_wd", [128, 16 * 1024], BF16))
    sg_sb = stB.enter_context(nc.sbuf_tensor("b2_sg", [128, 16 * 512], BF16))
    h_sb = sg_sb
    msk8 = stB.enter_context(nc.sbuf_tensor("b2_msk8", [8, 512], F32))
    cum8 = stB.enter_context(nc.sbuf_tensor("b2_cum8", [8, 512], F32))
    posx8 = cum8
    zs8 = stB.enter_context(nc.sbuf_tensor("b2_zs8", [8, 512], F32))
    ec_sb = stB.enter_context(nc.sbuf_tensor("b2_ec", [8, 1], F32))
    ovr8 = stB.enter_context(nc.sbuf_tensor("b2_ovr8", [8, 512], F32))
    posT_f = stB.enter_context(nc.sbuf_tensor("b2_posTf", [128, 32], F32))
    posT_i = stB.enter_context(nc.sbuf_tensor("b2_posTi", [128, 32], I32))
    xeT = stB.enter_context(nc.sbuf_tensor("b2_xeT", [128, 2 * 8 * CEXP], BF16))
    ye_sb = stB.enter_context(nc.sbuf_tensor("b2_ye", [128, 1024], BF16))
    ybuf = stB.enter_context(nc.sbuf_tensor("b2_ybuf", [128, 4 * 1024], BF16))
    zrow = stB.enter_context(nc.sbuf_tensor("b2_zrow", [128, 1024], BF16))
    acc_sb = stB.enter_context(nc.sbuf_tensor("b2_acc", [128, 4 * 1024], F32))
    logit_sb = stB.enter_context(nc.sbuf_tensor("b2_logit", [128, 32], F32))
    probs_sb = stB.enter_context(nc.sbuf_tensor("b2_probs", [128, 32], F32))
    gates_sb = stB.enter_context(nc.sbuf_tensor("b2_gates", [128, 32], F32))
    ge_sb = stB.enter_context(nc.sbuf_tensor("b2_ge", [128, 32], F32))
    tops_sb = stB.enter_context(nc.sbuf_tensor("b2_tops", [128, 8], F32))
    sp_sb = stB.enter_context(nc.sbuf_tensor("b2_sp", [128, 8], F32))
    rec_sb = stB.enter_context(nc.sbuf_tensor("b2_rec", [128, 4], F32))
    gT_sb = stB.enter_context(nc.sbuf_tensor("b2_gT", [8, 512], F32))
    ln_sb = stB.enter_context(nc.sbuf_tensor("b2_ln", [128, 16], F32))
    ones_sb = stB.enter_context(nc.sbuf_tensor("b2_ones", [1, 128], F32))

    @block.sync
    def _(sy):
        ls.dma(sy.dma_start(out=eye_f[:, :], in_=eye))
        ls.dma(sy.dma_start(out=oh_sb[:, :], in_=onehot))
        ls.dma(sy.dma_start(out=rb_sb[:, :], in_=rbias))
        ls.dma(sy.dma_start(out=rT_sb[:, :].rearrange("p (c e) -> p c e", c=8),
                            in_=routerT.rearrange("(c p) e -> p c e", p=128)))

    @block.gpsimd
    def _(g):
        ld.dma(g.dma_start(out=eye_bf[:, :], in_=eye))

    @block.sync
    def _(sy):
        ls.dma(sy.dma_start(out=ec_sb[:, :], in_=io["ecol"]))

    @block.vector
    def _(v):
        ve.inc(v.memset(zs8[:, :], 0.0))
        ve.inc(v.memset(ybuf[:, :], 0.0))
        ve.inc(v.memset(zrow[:, :], 0.0))
    ve_z = ve.n

    @block.gpsimd
    def _(g):
        g.wait_ge(ve.h, ve_z)
        for r in range(8 * NTE):
            ld.dma(g.dma_start(out=xe_all[r * 128:(r + 1) * 128, :],
                               in_=zrow[:, :]))

    E.b2_xeld = ls.n
    E.b2_peT = pe.n
    E.b2_gath = ld.n
    E.b2_xe_used = [pe.n, pe.n]
    E.b2_xe_cp = [ve.n, ve.n]
    cast_done = E.cast_done

    @block.vector
    def _(v):
        ve.inc(v.memset(ones_sb[:, :], 1.0))

    @block.tensor
    def _(te):
        ls.wait(te)
        ve.wait(te)
        pe.inc(te.matmul(pB[:, 0:8], ones_sb[:, :], rb_sb[:, :],
                         start=True, stop=True))

    @block.vector
    def _(v):
        pe.wait(v)
        ve.inc(v.tensor_copy(rbbc_sb[:, :], pB[:, 0:8]))

    tr_guard = 0
    for grp in range(NGRP):
        @block.gpsimd
        def _(g):
            g.wait_ge(pe.h, tr_guard)
            ve.wait(g)
            se.wait(g)
            so.wait(g)
            for t in range(4):
                ld.dma(g.dma_start(
                    out=x_tm[:, t * 1024:(t + 1) * 1024],
                    in_=xq[grp * 512 + t * 128: grp * 512 + (t + 1) * 128, :]))
                ld.dma(g.dma_start(
                    out=xf_sb[:, t * 1024:(t + 1) * 1024],
                    in_=xq[grp * 512 + t * 128: grp * 512 + (t + 1) * 128, :]))
        ld_x = ld.n
        ve_base = ve.n
        pe_base = pe.n

        @block.tensor
        def _(te):
            te.wait_ge(ld.h, ld_x)
            te.wait_ge(ve.h, ve_base)
            for c in range(8):
                if c >= 1:
                    te.wait_ge(ve.h, ve_base + c)
                for t in range(4):
                    inst = te.transpose(
                        pT0[:, t * 128:(t + 1) * 128],
                        x_tm[:, t * 1024 + c * 128: t * 1024 + (c + 1) * 128],
                        eye_bf[:, :])
                pe.inc(inst)

        @block.vector
        def _(v):
            for c in range(8):
                v.wait_ge(pe.h, pe_base + c + 1)
                ve.inc(v.tensor_copy(xT[:, c * 512:(c + 1) * 512], pT0[:, :]))

        for c in range(8):
            ve_c = ve.n
            pe_c = pe.n

            @block.tensor
            def _(te):
                te.wait_ge(ve.h, ve_c)
                for t in range(4):
                    inst = te.transpose(
                        pB[:, t * 128:(t + 1) * 128],
                        xf_sb[:, t * 1024 + c * 128: t * 1024 + (c + 1) * 128],
                        eye_f[:, :])
                pe.inc(inst)

            @block.vector
            def _(v):
                v.wait_ge(pe.h, pe_c + 1)
                ve.inc(v.tensor_copy(xc_sb[:, :], pB[:, :]))
            ve_c2 = ve.n

            @block.tensor
            def _(te):
                te.wait_ge(ve.h, ve_c2)
                for t in range(4):
                    inst = te.matmul(pGU[:, t * 512: t * 512 + 8],
                                     xc_sb[:, t * 128:(t + 1) * 128],
                                     rT_sb[:, c * 8:(c + 1) * 8],
                                     start=(c == 0), stop=(c == 7))
                pe.inc(inst)

        @block.vector
        def _(v):
            pe.wait(v)
            ve.inc(v.tensor_tensor(
                logit_sb[:, :].rearrange("p (a b) -> p a b", a=4),
                pGU[:, :].rearrange("p (a b) -> p a b", a=4)[:, :, 0:8],
                rbbc_sb[:, 0:8].rearrange("p (a b) -> p a b", a=1)
                    .broadcast_to([128, 4, 8]),
                ALU.add))
        ve_logit = ve.n

        @block.scalar
        def _(s):
            ve.wait(s)
            se.inc(s.activation(probs_sb[:, :], logit_sb[:, :], AF.Sigmoid))

        @block.vector
        def _(v):
            se.wait(v)
            for t in range(4):
                ve.inc(v.reduce_max(tops_sb[:, t:t + 1],
                                    logit_sb[:, t * 8:(t + 1) * 8], axis=AX.X))
            ve.wait(v)
            for t in range(4):
                ve.inc(v.tensor_scalar(ge_sb[:, t * 8:(t + 1) * 8],
                                       logit_sb[:, t * 8:(t + 1) * 8],
                                       tops_sb[:, t:t + 1], None, ALU.is_ge))
            ve.wait(v)
            for t in range(4):
                ve.inc(v.scalar_tensor_tensor(ge_sb[:, t * 8:(t + 1) * 8],
                                              ge_sb[:, t * 8:(t + 1) * 8], NEG,
                                              logit_sb[:, t * 8:(t + 1) * 8],
                                              ALU.mult, ALU.add))
            ve.wait(v)
            for t in range(4):
                ve.inc(v.reduce_max(tops_sb[:, 4 + t:5 + t],
                                    ge_sb[:, t * 8:(t + 1) * 8], axis=AX.X))

        @block.scalar
        def _(s):
            ve.wait(s)
            se.inc(s.activation(sp_sb[:, :], tops_sb[:, :], AF.Sigmoid))

        @block.vector
        def _(v):
            se.wait(v)
            for t in range(4):
                ve.inc(v.tensor_add(rec_sb[:, t:t + 1], sp_sb[:, t:t + 1],
                                    sp_sb[:, 4 + t:5 + t]))
            ve.wait(v)
            for t in range(4):
                ve.inc(v.reciprocal(rec_sb[:, t:t + 1], rec_sb[:, t:t + 1]))
            ve.wait(v)
            for t in range(4):
                ve.inc(v.tensor_scalar(ge_sb[:, t * 8:(t + 1) * 8],
                                       logit_sb[:, t * 8:(t + 1) * 8],
                                       tops_sb[:, 4 + t:5 + t], None, ALU.is_ge))
            ve.wait(v)
            for t in range(4):
                ve.inc(v.scalar_tensor_tensor(gates_sb[:, t * 8:(t + 1) * 8],
                                              probs_sb[:, t * 8:(t + 1) * 8],
                                              rec_sb[:, t:t + 1],
                                              ge_sb[:, t * 8:(t + 1) * 8],
                                              ALU.mult, ALU.mult))

        ve_g = ve.n
        pe_g = pe.n

        @block.tensor
        def _(te):
            te.wait_ge(ve.h, ve_g)
            for t in range(4):
                inst = te.transpose(pB[0:8, t * 128:(t + 1) * 128],
                                    gates_sb[:, t * 8:(t + 1) * 8], eye_f[:, :])
            pe.inc(inst)

        @block.vector
        def _(v):
            v.wait_ge(pe.h, pe_g + 1)
            ve.inc(v.tensor_copy(gT_sb[:, :], pB[0:8, :]))

        @block.vector
        def _(v):
            ve.wait(v)
            ve.inc(v.tensor_single_scalar(msk8[:, :], gT_sb[:, :], 0.0, ALU.is_gt))
            ve.wait(v)
            ve.inc(v.tensor_tensor_scan(cum8[:, :], msk8[:, :], zs8[:, :], 0.0,
                                        ALU.add, ALU.add))
            ve.wait(v)
            ve.inc(v.tensor_sub(cum8[:, :], cum8[:, :], msk8[:, :]))
            ve.wait(v)
            ve.inc(v.tensor_single_scalar(ovr8[:, :], cum8[:, :], float(CEXP),
                                          ALU.is_ge))
            ve.wait(v)
            ve.inc(v.tensor_scalar(cum8[:, :], cum8[:, :], ec_sb[:, 0:1], 1e6,
                                   ALU.add, ALU.add))
            ve.wait(v)
            ve.inc(v.scalar_tensor_tensor(posx8[:, :], msk8[:, :], -1e6,
                                          cum8[:, :], ALU.mult, ALU.add))
            ve.wait(v)
            ve.inc(v.scalar_tensor_tensor(posx8[:, :], ovr8[:, :], 2e6,
                                          posx8[:, :], ALU.mult, ALU.add))
        ve_pos = ve.n
        pe_pt0 = pe.n

        @block.tensor
        def _(te):
            te.wait_ge(ve.h, ve_pos)
            for t in range(4):
                inst = te.transpose(pB[:, t * 8:(t + 1) * 8],
                                    posx8[:, t * 128:(t + 1) * 128],
                                    eye_f[0:8, 0:8])
            pe.inc(inst)

        @block.vector
        def _(v):
            v.wait_ge(pe.h, pe_pt0 + 1)
            v.wait_ge(ld.h, E.b2_gath)
            ve.inc(v.tensor_copy(posT_f[:, :], pB[:, 0:32]))
            ve.wait(v)
            ve.inc(v.tensor_copy(posT_i[:, :], posT_f[:, :]))
        ve_pti = ve.n

        xeld_prev = E.b2_xeld

        @block.gpsimd
        def _(g):
            g.wait_ge(ve.h, ve_pti)
            g.wait_ge(ls.h, xeld_prev)
            for t in range(4):
                for e in range(8):
                    col = t * 8 + e
                    ld.dma(g.indirect_dma_start(
                        out=xe_all,
                        out_offset=bass.IndirectOffsetOnAxis(
                            posT_i[:, col:col + 1], 0),
                        in_=x_tm[:, t * 1024:(t + 1) * 1024], in_offset=None))
        scat_done = ld.n

        def emit_xe_prep(e):
            slot = e % 2
            xtb = slot * 8 * CEXP

            @block.sync
            def _(sy):
                sy.wait_ge(ld.h, scat_done)
                sy.wait_ge(pe.h, E.b2_xe_used[slot])
                for c in range(8):
                    ls.dma(sy.dma_start(
                        out=xeT[:, xtb + c * CEXP: xtb + (c + 1) * CEXP],
                        in_=xe_all[e * CEXP:(e + 1) * CEXP,
                                   c * 128:(c + 1) * 128],
                        transpose=True))
            E.b2_xe_cp[slot] = ls.n

        eseq = [8] + list(range(8))
        emit_xe_prep(0)
        if grp == 0:
            @block.sync
            def _(sy):
                sy.wait_ge(ld.h, cast_done[8])
                for c in range(8):
                    ls.dma(sy.dma_start(out=wbuf[:, c * 2048:(c + 1) * 2048],
                                        in_=wbg[8, c * 128:(c + 1) * 128, :]))
                    ls.dma(sy.dma_start(
                        out=wbuf[:, 16384 + c * 2048: 16384 + (c + 1) * 2048],
                        in_=wbu[8, c * 128:(c + 1) * 128, :]))
            E.b2_ld_gu = ls.n

        for ei, e in enumerate(eseq):
            routed = (e != 8)
            W = CEXP if routed else 512
            pe_e0 = pe.n
            ld_gu = E.b2_ld_gu

            if routed:
                slot = e % 2
                rhs_buf = xeT[:, slot * 8 * CEXP:(slot + 1) * 8 * CEXP]
                rhs_w = CEXP
                xe_cp = E.b2_xe_cp[slot]
            else:
                rhs_buf, rhs_w = xT, 512
                xe_cp = 0

            se_gb = se.n
            pe_gb = pe.n
            ve_gu_guard = ve_logit if ei == 0 else ve.n

            @block.tensor
            def _(te):
                te.wait_ge(ls.h, ld_gu)
                se.wait(te)
                te.wait_ge(ve.h, ve_gu_guard)
                if routed:
                    te.wait_ge(ls.h, xe_cp)
                for oc in range(16):
                    if oc >= 2:
                        te.wait_ge(se.h, se_gb + oc - 1)
                    for c in range(8):
                        inst = te.matmul(
                            pGU[:, (oc % 2) * 512: (oc % 2) * 512 + W],
                            wbuf[:, c * 2048 + oc * 128: c * 2048 + oc * 128 + 128],
                            rhs_buf[:, c * rhs_w:(c + 1) * rhs_w],
                            start=(c == 0), stop=(c == 7))
                    pe.inc(inst)

            @block.sync
            def _(sy):
                sy.wait_ge(ld.h, cast_done[e])
                sy.wait_ge(pe.h, pe_e0)
                for i in range(16):
                    ls.dma(sy.dma_start(out=wd_sb[:, i * 1024:(i + 1) * 1024],
                                        in_=wbd[e, i * 128:(i + 1) * 128, :]))
            ld_d = ls.n

            @block.scalar
            def _(s):
                for oc in range(16):
                    s.wait_ge(pe.h, pe_gb + oc + 1)
                    se.inc(s.activation(sg_sb[:, oc * W:(oc + 1) * W],
                                        pGU[:, (oc % 2) * 512: (oc % 2) * 512 + W],
                                        AF.Silu))
            se_gu = se.n

            ve_ub = ve.n
            pe_ub = pe.n

            @block.tensor
            def _(te):
                te.wait_ge(ve.h, ve_gu_guard if ei == 0 else ve_ub)
                for oc in range(16):
                    if oc >= 2:
                        te.wait_ge(ve.h, ve_ub + oc - 1)
                    for c in range(8):
                        inst = te.matmul(
                            pGU[:, (2 + oc % 2) * 512: (2 + oc % 2) * 512 + W],
                            wbuf[:, 16384 + c * 2048 + oc * 128:
                                 16384 + c * 2048 + oc * 128 + 128],
                            rhs_buf[:, c * rhs_w:(c + 1) * rhs_w],
                            start=(c == 0), stop=(c == 7))
                    pe.inc(inst)
            pe_after_gu = pe.n
            if routed:
                E.b2_xe_used[e % 2] = pe.n
                if e + 1 <= 7:
                    emit_xe_prep(e + 1)

            nxt = None
            if ei + 1 < len(eseq):
                nxt = eseq[ei + 1]
            elif grp + 1 < NGRP:
                nxt = 8
            if nxt is not None:
                @block.sync
                def _(sy):
                    sy.wait_ge(ld.h, cast_done[nxt])
                    sy.wait_ge(pe.h, pe_after_gu)
                    for c in range(8):
                        ls.dma(sy.dma_start(out=wbuf[:, c * 2048:(c + 1) * 2048],
                                            in_=wbg[nxt, c * 128:(c + 1) * 128, :]))
                        ls.dma(sy.dma_start(
                            out=wbuf[:, 16384 + c * 2048: 16384 + (c + 1) * 2048],
                            in_=wbu[nxt, c * 128:(c + 1) * 128, :]))
                E.b2_ld_gu = ls.n

            @block.vector
            def _(v):
                v.wait_ge(se.h, se_gu)
                for oc in range(16):
                    v.wait_ge(pe.h, pe_ub + oc + 1)
                    ve.inc(v.tensor_tensor(
                        h_sb[:, oc * W:(oc + 1) * W],
                        pGU[:, (2 + oc % 2) * 512: (2 + oc % 2) * 512 + W],
                        sg_sb[:, oc * W:(oc + 1) * W],
                        ALU.mult))
            ve_hb = ve.n

            if routed:
                for tt in range(NTE):
                    ve_pre = ve.n

                    @block.tensor
                    def _(te):
                        te.wait_ge(ls.h, ld_d)
                        te.wait_ge(ve.h, ve_pre)
                        for i in range(16):
                            for hh in range(2):
                                inst = te.matmul(
                                    pY[:, hh * 512:(hh + 1) * 512],
                                    h_sb[:, i * CEXP + tt * 128:
                                         i * CEXP + tt * 128 + 128],
                                    wd_sb[:, i * 1024 + hh * 512:
                                          i * 1024 + (hh + 1) * 512],
                                    start=(i == 0), stop=(i == 15))
                        pe.inc(inst)
                    pv_tt = pe.n

                    @block.vector
                    def _(v):
                        v.wait_ge(pe.h, pv_tt)
                        if tt > 0:
                            so.wait(v)
                        ve.inc(v.tensor_copy(ye_sb[:, :], pY[:, :]))
                    ve_evac = ve.n

                    @block.sync
                    def _(sy):
                        sy.wait_ge(ve.h, ve_evac)
                        so.dma(sy.dma_start(
                            out=ye_all[e * CEXP + tt * 128:
                                       e * CEXP + (tt + 1) * 128, :],
                            in_=ye_sb[:, :]))
                so_y = so.n

                ve_accprev = ve.n

                @block.gpsimd
                def _(g):
                    g.wait_ge(so.h, so_y)
                    g.wait_ge(ve.h, ve_accprev)
                    for t in range(4):
                        col = t * 8 + e
                        ld.dma(g.indirect_dma_start(
                            out=ybuf[:, t * 1024:(t + 1) * 1024], out_offset=None,
                            in_=ye_all,
                            in_offset=bass.IndirectOffsetOnAxis(
                                posT_i[:, col:col + 1], 0)))
                ld_gath = ld.n
                E.b2_gath = ld.n

                @block.vector
                def _(v):
                    v.wait_ge(ld.h, ld_gath)
                    for t in range(4):
                        col = t * 8 + e
                        ve.inc(v.scalar_tensor_tensor(
                            acc_sb[:, t * 1024:(t + 1) * 1024],
                            ybuf[:, t * 1024:(t + 1) * 1024],
                            gates_sb[:, col:col + 1],
                            acc_sb[:, t * 1024:(t + 1) * 1024],
                            ALU.mult, ALU.add))
            else:
                ve_yb = ve.n
                pe_yb = pe.n
                for t in range(4):
                    @block.tensor
                    def _(te):
                        te.wait_ge(ls.h, ld_d)
                        te.wait_ge(ve.h, ve_hb)
                        if t > 0:
                            te.wait_ge(ve.h, ve_yb + t)
                        for i in range(16):
                            for hh in range(2):
                                inst = te.matmul(
                                    pY[:, hh * 512:(hh + 1) * 512],
                                    h_sb[:, i * 512 + t * 128: i * 512 + t * 128 + 128],
                                    wd_sb[:, i * 1024 + hh * 512:
                                          i * 1024 + (hh + 1) * 512],
                                    start=(i == 0), stop=(i == 15))
                        pe.inc(inst)

                    @block.vector
                    def _(v):
                        v.wait_ge(pe.h, pe_yb + t + 1)
                        ve.inc(v.tensor_copy(acc_sb[:, t * 1024:(t + 1) * 1024],
                                             pY[:, :]))

        tr_guard = pe.n

        for t in range(4):
            @block.vector
            def _(v):
                ve.wait(v)
                ve.inc(v.tensor_tensor(acc_sb[:, t * 1024:(t + 1) * 1024],
                                       acc_sb[:, t * 1024:(t + 1) * 1024],
                                       xf_sb[:, t * 1024:(t + 1) * 1024], ALU.add))
                ve.wait(v)
                ve.inc(v.reduce_sum(ln_sb[:, t:t + 1],
                                    acc_sb[:, t * 1024:(t + 1) * 1024], axis=AX.X))
                ve.wait(v)
                ve.inc(v.tensor_scalar_mul(ln_sb[:, 4 + t:5 + t], ln_sb[:, t:t + 1],
                                           -1.0 / 1024.0))
                ve.wait(v)
                ve.inc(v.tensor_scalar_add(acc_sb[:, t * 1024:(t + 1) * 1024],
                                           acc_sb[:, t * 1024:(t + 1) * 1024],
                                           ln_sb[:, 4 + t:5 + t]))

            @block.scalar
            def _(s):
                ve.wait(s)
                se.inc(s.activation(x_tm[:, 0:1024],
                                    acc_sb[:, t * 1024:(t + 1) * 1024],
                                    AF.Square, accum_out=ln_sb[:, t:t + 1]))

            @block.vector
            def _(v):
                se.wait(v)
                ve.inc(v.tensor_scalar(ln_sb[:, 8 + t:9 + t], ln_sb[:, t:t + 1],
                                       1.0 / 1024.0, LN_EPS, ALU.mult, ALU.add))

            @block.scalar
            def _(s):
                ve.wait(s)
                se.inc(s.activation(ln_sb[:, 8 + t:9 + t], ln_sb[:, 8 + t:9 + t],
                                    AF.Sqrt))

            @block.vector
            def _(v):
                se.wait(v)
                ve.inc(v.reciprocal(ln_sb[:, 12 + t:13 + t], ln_sb[:, 8 + t:9 + t]))
                ve.wait(v)
                ve.inc(v.tensor_scalar_mul(xf_sb[:, t * 1024:(t + 1) * 1024],
                                           acc_sb[:, t * 1024:(t + 1) * 1024],
                                           ln_sb[:, 12 + t:13 + t]))

            @block.sync
            def _(sy):
                ve.wait(sy)
                so.dma(sy.dma_start(
                    out=x2q[grp * 512 + t * 128: grp * 512 + (t + 1) * 128, :],
                    in_=xf_sb[:, t * 1024:(t + 1) * 1024]))

            @block.gpsimd
            def _(g):
                ve.wait(g)
                so.dma(g.dma_start(
                    out=x2bf[grp * 512 + t * 128: grp * 512 + (t + 1) * 128, :],
                    in_=xf_sb[:, t * 1024:(t + 1) * 1024]))

        @block.gpsimd
        def _(g):
            so.wait(g)
            E.cc.inc(g.collective_compute(
                "AllGather", ALU.bypass, replica_groups=[list(range(NCORE))],
                ins=[x2bf[grp * 512:(grp + 1) * 512, :].opt()],
                outs=[xg[grp * NCORE * 512:(grp + 1) * NCORE * 512, :].opt()],
                output_never_consumed=True), 1)

    return stB



def emit_l1_attn(E, io):
    nc, block = E.nc, E.block
    ld, ls, pe, ve, se, so, cc = E.ld, E.ls, E.pe, E.ve, E.se, E.so, E.cc
    cg = E.cg
    pS, pPV, pT0 = E.pS, E.pPV, E.pT0
    xg, xgb, xq = io["xg"], io["xgb"], io["x2q_buf"]
    WqT, WkT, WvT = io["WqT"], io["WkT"], io["WvT"]
    W1T, W2T, W3T = io["W1T"], io["W2T"], io["W3T"]
    bq_pc, bk_pc = io["bq_pc"], io["bk_pc"]
    b1_pc, b2_pc = io["b1_pc"], io["b2_pc"]
    bv_row, b3_row = io["bv_row"], io["b3_row"]
    eye = io["eye"]
    x3q = io["x3q_buf"]
    kT_buf, v_buf, qT_buf = io["kT_buf"], io["v_buf"], io["qT_buf"]
    w1bf, w2bf, w3bf = io["w1bf"], io["w2bf"], io["w3bf"]
    cc_base = cc.n - NGRP

    stB = ExitStack()
    pT1 = stB.enter_context(nc.psum_tensor("b3_pT1", [128, 512], BF16))
    pTt = [pT0, pT1]
    eye_bf = stB.enter_context(nc.sbuf_tensor("b3_eye", [128, 128], BF16))
    stat = stB.enter_context(nc.sbuf_tensor("b3_stat", [128, 24], F32))
    ln_sb = stB.enter_context(nc.sbuf_tensor("b3_ln", [128, 16], F32))
    ones_r = stB.enter_context(nc.sbuf_tensor("b3_onesr", [1, 128], F32))
    ones_c = stB.enter_context(nc.sbuf_tensor("b3_onesc", [128, 1], F32))
    b1s = stB.enter_context(nc.sbuf_tensor("b3_b1s", [128, 8], F32))
    b2s = stB.enter_context(nc.sbuf_tensor("b3_b2s", [128, 32], F32))
    b3r = stB.enter_context(nc.sbuf_tensor("b3_b3r", [1, D], F32))
    b3b = stB.enter_context(nc.sbuf_tensor("b3_b3b", [128, D], BF16))

    stA = ExitStack()
    wq_sb = stA.enter_context(nc.sbuf_tensor("b3_wq", [128, 8 * 1024], BF16))
    wk_sb = stA.enter_context(nc.sbuf_tensor("b3_wk", [128, 8 * 1024], BF16))
    wv_sb = stA.enter_context(nc.sbuf_tensor("b3_wv", [128, 8 * 1024], BF16))
    x_tm = stA.enter_context(nc.sbuf_tensor("b3_xtm", [128, 4 * 1024], BF16))
    xT = stA.enter_context(nc.sbuf_tensor("b3_xT", [128, 8 * 512], BF16))
    kf_sb = stA.enter_context(nc.sbuf_tensor("b3_kf", [128, 8 * 512], F32))
    ksq_sb = stA.enter_context(nc.sbuf_tensor("b3_ksq", [128, 2 * 512], F32))
    rk_row = stA.enter_context(nc.sbuf_tensor("b3_rk", [1, 512], F32))
    rkb_sb = stA.enter_context(nc.sbuf_tensor("b3_rkb", [128, 512], F32))
    kv_stage = stA.enter_context(nc.sbuf_tensor("b3_kvst", [128, 8 * 512], BF16))
    v_stage = stA.enter_context(nc.sbuf_tensor("b3_vst", [128, 4 * 1024], BF16))
    bq_s = stA.enter_context(nc.sbuf_tensor("b3_bqs", [128, 8], F32))
    bk_s = stA.enter_context(nc.sbuf_tensor("b3_bks", [128, 8], F32))
    bv_r = stA.enter_context(nc.sbuf_tensor("b3_bvr", [1, D], F32))
    bvb_sb = stA.enter_context(nc.sbuf_tensor("b3_bvb", [128, D], F32))

    @block.gpsimd
    def _(g):
        E.bsel = g.partition_id() // GPB
        for w_sb, src in ((wq_sb, WqT), (wk_sb, WkT), (wv_sb, WvT)):
            for c in range(8):
                ld.dma(g.dma_start(out=w_sb[:, c * 1024:(c + 1) * 1024],
                                   in_=src[c * 128:(c + 1) * 128, :]))
        ld.dma(g.dma_start(out=eye_bf[:, :], in_=eye))
    mlp_cast = E.mlp_cast

    @block.sync
    def _(sy):
        ls.dma(sy.dma_start(out=bq_s[:, :], in_=bq_pc))
        ls.dma(sy.dma_start(out=bk_s[:, :], in_=bk_pc))
        ls.dma(sy.dma_start(out=b1s[:, :], in_=b1_pc))
        ls.dma(sy.dma_start(out=b2s[:, :], in_=b2_pc))
        ls.dma(sy.dma_start(out=bv_r[:, :], in_=bv_row))
        ls.dma(sy.dma_start(out=b3r[:, :], in_=b3_row))

    @block.vector
    def _(v):
        ve.inc(v.memset(ones_r[:, :], 1.0))
        ve.inc(v.memset(ones_c[:, :], 1.0))

    @block.tensor
    def _(te):
        ls.wait(te)
        ve.wait(te)
        for h in range(2):
            pe.inc(te.matmul(pS[:, h * 512:(h + 1) * 512], ones_r[:, :],
                             bv_r[:, h * 512:(h + 1) * 512], start=True, stop=True))
        for h in range(2):
            pe.inc(te.matmul(pS[:, 1024 + h * 512: 1024 + (h + 1) * 512],
                             ones_r[:, :], b3r[:, h * 512:(h + 1) * 512],
                             start=True, stop=True))
    pe_bc = pe.n

    @block.vector
    def _(v):
        v.wait_ge(pe.h, pe_bc)
        ve.inc(v.tensor_copy(bvb_sb[:, :], pS[:, 0:1024]))
        ve.inc(v.tensor_copy(b3b[:, :], pS[:, 1024:2048]))

    def emit_group_xT(src, grp, pe_guard, gathered=False):
        @block.gpsimd
        def _(g):
            g.wait_ge(pe.h, pe_guard)
            if gathered:
                cg.wait(g, 16 * (grp // 4 + 1))
            for t in range(4):
                ld.dma(g.dma_start(
                    out=x_tm[:, t * 1024:(t + 1) * 1024],
                    in_=src[grp * 512 + t * 128: grp * 512 + (t + 1) * 128, :]))
        ld_now = ld.n
        ve_base = ve.n
        pe_base = pe.n

        @block.tensor
        def _(te):
            te.wait_ge(ld.h, ld_now)
            te.wait_ge(ve.h, ve_base)
            for c in range(8):
                par = pTt[c % 2]
                if c >= 2:
                    te.wait_ge(ve.h, ve_base + c - 1)
                for t in range(4):
                    inst = te.transpose(
                        par[:, t * 128:(t + 1) * 128],
                        x_tm[:, t * 1024 + c * 128: t * 1024 + (c + 1) * 128],
                        eye_bf[:, :])
                pe.inc(inst)

        @block.vector
        def _(v):
            for c in range(8):
                v.wait_ge(pe.h, pe_base + c + 1)
                ve.inc(v.tensor_copy(xT[:, c * 512:(c + 1) * 512], pTt[c % 2][:, :]))
        return pe.n

    def emit_norm_proj(w_sb, bias_s, dst_buf, grp):
        ve_now = ve.n
        se_base = se.n
        pe_base = pe.n

        @block.tensor
        def _(te):
            te.wait_ge(ve.h, ve_now)
            te.wait_ge(se.h, se_base)
            for oc in range(8):
                if oc >= 3:
                    te.wait_ge(se.h, se_base + oc - 2)
                for c in range(8):
                    inst = te.matmul(
                        pS[:, (oc % 3) * 512:(oc % 3 + 1) * 512],
                        w_sb[:, c * 1024 + oc * 128: c * 1024 + oc * 128 + 128],
                        xT[:, c * 512:(c + 1) * 512],
                        start=(c == 0), stop=(c == 7))
                pe.inc(inst)

        @block.scalar
        def _(s):
            for oc in range(8):
                s.wait_ge(pe.h, pe_base + oc + 1)
                se.inc(s.activation(kf_sb[:, oc * 512:(oc + 1) * 512],
                                    pS[:, (oc % 3) * 512:(oc % 3 + 1) * 512],
                                    AF.Identity, bias=bias_s[:, oc:oc + 1]))
        se_evac = se.n

        pe_sq = pe.n
        se_sq0 = se.n

        @block.scalar
        def _(s):
            s.wait_ge(se.h, se_evac)
            for oc in range(8):
                if oc >= 2:
                    s.wait_ge(pe.h, pe_sq + oc - 1)
                se.inc(s.activation(ksq_sb[:, (oc % 2) * 512:(oc % 2 + 1) * 512],
                                    kf_sb[:, oc * 512:(oc + 1) * 512], AF.Square))

        @block.tensor
        def _(te):
            for oc in range(8):
                te.wait_ge(se.h, se_sq0 + oc + 1)
                inst = te.matmul(pS[0:1, 3 * 512:3 * 512 + 512],
                                 ones_c[:, :],
                                 ksq_sb[:, (oc % 2) * 512:(oc % 2 + 1) * 512],
                                 start=(oc == 0), stop=(oc == 7))
                pe.inc(inst)
        pe_ssq = pe.n

        @block.vector
        def _(v):
            v.wait_ge(pe.h, pe_ssq)
            ve.inc(v.tensor_copy(rk_row[:, :], pS[0:1, 3 * 512:3 * 512 + 512]))

        @block.scalar
        def _(s):
            ve.wait(s)
            se.inc(s.activation(rk_row[:, :], rk_row[:, :], AF.Sqrt))

        @block.vector
        def _(v):
            se.wait(v)
            ve.inc(v.tensor_scalar_max(rk_row[:, :], rk_row[:, :], 1e-12))
            ve.wait(v)
            ve.inc(v.reciprocal(rk_row[:, :], rk_row[:, :]))
        ve_rk = ve.n

        @block.tensor
        def _(te):
            te.wait_ge(ve.h, ve_rk)
            pe.inc(te.matmul(pS[:, 3 * 512:4 * 512], ones_r[:, :], rk_row[:, :],
                             start=True, stop=True))
        pe_rkb = pe.n

        @block.vector
        def _(v):
            v.wait_ge(pe.h, pe_rkb)
            ve.inc(v.tensor_copy(rkb_sb[:, :], pS[:, 3 * 512:4 * 512]))
            ve.wait(v)
            for oc in range(8):
                ve.inc(v.tensor_tensor(kv_stage[:, oc * 512:(oc + 1) * 512],
                                       kf_sb[:, oc * 512:(oc + 1) * 512],
                                       rkb_sb[:, :], ALU.mult))
        ve_st = ve.n

        @block.sync
        def _(sy):
            sy.wait_ge(ve.h, ve_st)
            for oc in range(8):
                so.dma(sy.dma_start(
                    out=dst_buf[oc * 128:(oc + 1) * 128, grp * 512:(grp + 1) * 512],
                    in_=kv_stage[:, oc * 512:(oc + 1) * 512]))

    tr_guard = 0
    for grp in range(32):
        if grp % 4 == 0:
            chunk = grp // 4
            @block.gpsimd
            def _(g):
                g.wait_ge(cc.h, cc_base + chunk + 1)
                cg.dma(g.dma_start(
                    out=xgb[chunk * 2048:(chunk + 1) * 2048, :],
                    in_=xg[bass.ts(E.bsel + 2 * chunk, 2048), :]))
        tr_guard = emit_group_xT(xgb, grp, tr_guard, gathered=True)
        emit_norm_proj(wk_sb, bk_s, kT_buf, grp)

        ve_vbase = ve.n
        pe_vbase = pe.n
        for t in range(4):
            @block.tensor
            def _(te):
                if t == 0:
                    te.wait_ge(ve.h, ve_vbase)
                else:
                    te.wait_ge(ve.h, ve_vbase + t)
                for c in range(8):
                    for h in range(2):
                        inst = te.matmul(
                            pPV[:, h * 512:(h + 1) * 512],
                            xT[:, c * 512 + t * 128: c * 512 + t * 128 + 128],
                            wv_sb[:, c * 1024 + h * 512: c * 1024 + (h + 1) * 512],
                            start=(c == 0), stop=(c == 7))
                pe.inc(inst)

            @block.vector
            def _(v):
                v.wait_ge(pe.h, pe_vbase + t + 1)
                ve.inc(v.tensor_tensor(v_stage[:, t * 1024:(t + 1) * 1024],
                                       pPV[:, :], bvb_sb[:, :], ALU.add))

        ve_v = ve.n

        @block.sync
        def _(sy):
            sy.wait_ge(ve.h, ve_v)
            for t in range(4):
                so.dma(sy.dma_start(
                    out=v_buf[grp * 512 + t * 128: grp * 512 + (t + 1) * 128, :],
                    in_=v_stage[:, t * 1024:(t + 1) * 1024]))

        @block.tensor
        def _(te):
            so.wait(te)

    for grp in range(8):
        tr_guard = emit_group_xT(xq, grp, tr_guard)
        emit_norm_proj(wq_sb, bq_s, qT_buf, grp)

        @block.tensor
        def _(te):
            so.wait(te)

    stA.close()
    qT_sb = stB.enter_context(nc.sbuf_tensor("b3_qT", [128, 8 * 512], BF16))
    kTm_sb = stB.enter_context(nc.sbuf_tensor("b3_kTm", [128, 8 * 2048], BF16))
    vm_sb = stB.enter_context(nc.sbuf_tensor("b3_vm", [128, 16 * 1024], BF16))
    p_sb = stB.enter_context(nc.sbuf_tensor("b3_p", [128, 2048], BF16))
    pT_sb = stB.enter_context(nc.sbuf_tensor("b3_pT", [128, 16 * 128], BF16))
    O_sb = stB.enter_context(nc.sbuf_tensor("b3_O", [128, 4 * 1024], F32))
    Obf_sb = stB.enter_context(nc.sbuf_tensor("b3_Obf", [128, 4 * 1024], BF16))
    wbuf = stB.enter_context(nc.sbuf_tensor("b3_wbuf", [128, 32 * 1024], BF16))
    w1_sb = stB.enter_context(nc.sbuf_tensor("b3_w1", [128, 8 * 1024], BF16))
    xf_sb = stB.enter_context(nc.sbuf_tensor("b3_xf", [128, 4 * 1024], F32))

    st8 = dict(exp=[se.n, se.n], sts=[ve.n, ve.n], ptdone=[pe.n, pe.n],
               pvdone=[pe.n, pe.n], oup=ve.n, odone=ve.n, kvfree=[pe.n, pe.n],
               kvready=[ls.n, ls.n])
    for a in range(NGRP):
        @block.sync
        def _(sy):
            pe.wait(sy)
            so.wait(sy)
            for c in range(8):
                ls.dma(sy.dma_start(
                    out=qT_sb[:, c * 512:(c + 1) * 512],
                    in_=qT_buf[c * 128:(c + 1) * 128, a * 512:(a + 1) * 512]))

        @block.sync
        def _(sy):
            ve.wait(sy)
            se.wait(sy)
            so.wait(sy)
            for t in range(4):
                ls.dma(sy.dma_start(
                    out=xf_sb[:, t * 1024:(t + 1) * 1024],
                    in_=xq[a * 512 + t * 128: a * 512 + (t + 1) * 128, :]))
        ls_res = ls.n

        pe_wpre = pe.n

        @block.sync
        def _(sy):
            sy.wait_ge(ld.h, mlp_cast)
            sy.wait_ge(pe.h, pe_wpre)
            if a == 0:
                for c in range(8):
                    ls.dma(sy.dma_start(out=w1_sb[:, c * 1024:(c + 1) * 1024],
                                        in_=w1bf[c * 128:(c + 1) * 128, :]))
            for c in range(8):
                ls.dma(sy.dma_start(out=wbuf[:, c * 4096:(c + 1) * 4096],
                                    in_=w2bf[c * 128:(c + 1) * 128, :]))
        ld_w12 = ls.n

        emit_flash_group(E, a, dict(
            causal=False, SC=1.0, nmac=16,
            qT_sb=qT_sb, kTm_sb=kTm_sb, vm_sb=vm_sb, p_sb=p_sb, pT_sb=pT_sb,
            O_sb=O_sb, stat=stat, dm_sb=None, eye_bf=eye_bf, pTt=pTt,
            kT_buf=kT_buf, v_buf=v_buf), st8)

        @block.vector
        def _(v):
            ve.wait(v)
            se.wait(v)
            for t in range(4):
                ve.inc(v.reciprocal(stat[:, 12 + t:13 + t], stat[:, 4 + t:5 + t]))
            ve.wait(v)
            for t in range(4):
                ve.inc(v.tensor_scalar_mul(Obf_sb[:, t * 1024:(t + 1) * 1024],
                                           O_sb[:, t * 1024:(t + 1) * 1024],
                                           stat[:, 12 + t:13 + t]))

        ve_otb = ve.n
        pe_otb = pe.n

        @block.tensor
        def _(te):
            te.wait_ge(ve.h, ve_otb)
            for c in range(8):
                par = pTt[c % 2]
                if c >= 2:
                    te.wait_ge(ve.h, ve_otb + c - 1)
                for t in range(4):
                    inst = te.transpose(
                        par[:, t * 128:(t + 1) * 128],
                        Obf_sb[:, t * 1024 + c * 128: t * 1024 + (c + 1) * 128],
                        eye_bf[:, :])
                pe.inc(inst)

        @block.vector
        def _(v):
            for c in range(8):
                v.wait_ge(pe.h, pe_otb + c + 1)
                ve.inc(v.tensor_copy(vm_sb[:, 4096 + c * 512: 4096 + (c + 1) * 512],
                                     pTt[c % 2][:, :]))

        ve_h1 = ve.n
        se_h1b = se.n
        pe_h1b = pe.n

        @block.tensor
        def _(te):
            te.wait_ge(ls.h, ld_w12)
            te.wait_ge(ve.h, ve_h1)
            se.wait(te)
            for oc in range(8):
                if oc >= 4:
                    te.wait_ge(se.h, se_h1b + oc - 3)
                for c in range(8):
                    inst = te.matmul(
                        pS[:, (oc % 4) * 512:(oc % 4 + 1) * 512],
                        w1_sb[:, c * 1024 + oc * 128: c * 1024 + oc * 128 + 128],
                        vm_sb[:, 4096 + c * 512: 4096 + (c + 1) * 512],
                        start=(c == 0), stop=(c == 7))
                pe.inc(inst)

        @block.scalar
        def _(s):
            for oc in range(8):
                s.wait_ge(pe.h, pe_h1b + oc + 1)
                se.inc(s.activation(vm_sb[:, oc * 512:(oc + 1) * 512],
                                    pS[:, (oc % 4) * 512:(oc % 4 + 1) * 512],
                                    AF.Relu, bias=b1s[:, oc:oc + 1]))

        se_h2b = se.n
        pe_h2b = pe.n
        se_h1done = se.n

        @block.tensor
        def _(te):
            te.wait_ge(ls.h, ld_w12)
            te.wait_ge(se.h, se_h1done)
            for oc in range(32):
                if oc >= 4:
                    te.wait_ge(se.h, se_h2b + oc - 3)
                for c in range(8):
                    inst = te.matmul(
                        pS[:, (oc % 4) * 512:(oc % 4 + 1) * 512],
                        wbuf[:, c * 4096 + oc * 128: c * 4096 + oc * 128 + 128],
                        vm_sb[:, c * 512:(c + 1) * 512],
                        start=(c == 0), stop=(c == 7))
                pe.inc(inst)

        @block.scalar
        def _(s):
            for oc in range(32):
                s.wait_ge(pe.h, pe_h2b + oc + 1)
                se.inc(s.activation(kTm_sb[:, oc * 512:(oc + 1) * 512],
                                    pS[:, (oc % 4) * 512:(oc % 4 + 1) * 512],
                                    AF.Relu, bias=b2s[:, oc:oc + 1]))

        pe_w3g = pe.n

        @block.sync
        def _(sy):
            sy.wait_ge(pe.h, pe_w3g)
            for hc in range(32):
                ls.dma(sy.dma_start(out=wbuf[:, hc * 1024:(hc + 1) * 1024],
                                    in_=w3bf[hc * 128:(hc + 1) * 128, :]))
        ld_w3 = ls.n
        se_h2done = se.n
        ld_res = ls_res

        ev_mark = {}
        for t in range(4):
            @block.tensor
            def _(te):
                te.wait_ge(ls.h, ld_w3)
                te.wait_ge(se.h, se_h2done)
                if t == 0:
                    ve.wait(te)
                else:
                    te.wait_ge(ve.h, ev_mark[t - 1])
                for hc in range(32):
                    for h in range(2):
                        inst = te.matmul(
                            pPV[:, h * 512:(h + 1) * 512],
                            kTm_sb[:, hc * 512 + t * 128: hc * 512 + t * 128 + 128],
                            wbuf[:, hc * 1024 + h * 512: hc * 1024 + (h + 1) * 512],
                            start=(hc == 0), stop=(hc == 31))
                pe.inc(inst)
            pv_t = pe.n

            @block.vector
            def _(v):
                v.wait_ge(pe.h, pv_t)
                v.wait_ge(ls.h, ld_res)
                ve.inc(v.tensor_tensor(O_sb[:, t * 1024:(t + 1) * 1024], pPV[:, :],
                                       b3b[:, :], ALU.add))
            ev_mark[t] = ve.n

        for t in range(4):
            @block.vector
            def _(v):
                ve.wait(v)
                ve.inc(v.tensor_tensor(O_sb[:, t * 1024:(t + 1) * 1024],
                                       O_sb[:, t * 1024:(t + 1) * 1024],
                                       xf_sb[:, t * 1024:(t + 1) * 1024], ALU.add))
                ve.wait(v)
                ve.inc(v.reduce_sum(ln_sb[:, t:t + 1],
                                    O_sb[:, t * 1024:(t + 1) * 1024], axis=AX.X))
                ve.wait(v)
                ve.inc(v.tensor_scalar_mul(ln_sb[:, 4 + t:5 + t], ln_sb[:, t:t + 1],
                                           -1.0 / 1024.0))
                ve.wait(v)
                ve.inc(v.tensor_scalar_add(O_sb[:, t * 1024:(t + 1) * 1024],
                                           O_sb[:, t * 1024:(t + 1) * 1024],
                                           ln_sb[:, 4 + t:5 + t]))

            @block.scalar
            def _(s):
                ve.wait(s)
                se.inc(s.activation(Obf_sb[:, 0:1024],
                                    O_sb[:, t * 1024:(t + 1) * 1024],
                                    AF.Square, accum_out=ln_sb[:, t:t + 1]))

            @block.vector
            def _(v):
                se.wait(v)
                ve.inc(v.tensor_scalar(ln_sb[:, 8 + t:9 + t], ln_sb[:, t:t + 1],
                                       1.0 / 1024.0, LN_EPS, ALU.mult, ALU.add))

            @block.scalar
            def _(s):
                ve.wait(s)
                se.inc(s.activation(ln_sb[:, 8 + t:9 + t], ln_sb[:, 8 + t:9 + t],
                                    AF.Sqrt))

            @block.vector
            def _(v):
                se.wait(v)
                ve.inc(v.reciprocal(ln_sb[:, 12 + t:13 + t], ln_sb[:, 8 + t:9 + t]))
                ve.wait(v)
                ve.inc(v.tensor_scalar_mul(xf_sb[:, t * 1024:(t + 1) * 1024],
                                           O_sb[:, t * 1024:(t + 1) * 1024],
                                           ln_sb[:, 12 + t:13 + t]))

            @block.sync
            def _(sy):
                ve.wait(sy)
                so.dma(sy.dma_start(
                    out=x3q[a * 512 + t * 128: a * 512 + (t + 1) * 128, :],
                    in_=xf_sb[:, t * 1024:(t + 1) * 1024]))

    return stB



def emit_l1_ffn(E, io):
    nc, block = E.nc, E.block
    ld, ls, pe, ve, se, so = E.ld, E.ls, E.pe, E.ve, E.se, E.so
    pS, pPV, pT0 = E.pS, E.pPV, E.pT0
    xq = io["x3q_buf"]
    w1, w2 = io["f_w1"], io["f_w2"]
    b1_pc, b2_row = io["f_b1_pc"], io["f_b2_row"]
    eye = io["eye"]
    x4q = io["x4q"]
    bw1T_buf, bw2T_buf = io["bw1T_buf"], io["bw2T_buf"]

    stB = ExitStack()
    pT1 = stB.enter_context(nc.psum_tensor("b4_pT1", [128, 512], BF16))
    pTt = [pT0, pT1]
    eye_bf = stB.enter_context(nc.sbuf_tensor("b4_eye", [128, 128], BF16))
    ln_sb = stB.enter_context(nc.sbuf_tensor("b4_ln", [128, 16], F32))
    ones_r = stB.enter_context(nc.sbuf_tensor("b4_ones", [1, 128], F32))
    b1s = stB.enter_context(nc.sbuf_tensor("b4_b1s", [128, 32], F32))
    b2r = stB.enter_context(nc.sbuf_tensor("b4_b2r", [1, D], F32))
    b2b = stB.enter_context(nc.sbuf_tensor("b4_b2b", [128, D], F32))

    @block.sync
    def _(sy):
        ls.dma(sy.dma_start(out=b1s[:, :], in_=b1_pc))
        ls.dma(sy.dma_start(out=b2r[:, :], in_=b2_row))

    @block.gpsimd
    def _(g):
        ld.dma(g.dma_start(out=eye_bf[:, :], in_=eye))

    @block.vector
    def _(v):
        ve.inc(v.memset(ones_r[:, :], 1.0))

    @block.tensor
    def _(te):
        ls.wait(te)
        ve.wait(te)
        ld.wait(te)
        for h in range(2):
            pe.inc(te.matmul(pS[:, h * 512:(h + 1) * 512], ones_r[:, :],
                             b2r[:, h * 512:(h + 1) * 512], start=True, stop=True))
    pe_bc = pe.n

    @block.vector
    def _(v):
        v.wait_ge(pe.h, pe_bc)
        ve.inc(v.tensor_copy(b2b[:, :], pS[:, 0:1024]))

    st0 = ExitStack()
    wc_sb = st0.enter_context(nc.sbuf_tensor("b4_wc", [128, 4096], F32))
    sgn_sb = st0.enter_context(nc.sbuf_tensor("b4_sgn", [128, 4096], F32))
    bwc_sb = st0.enter_context(nc.sbuf_tensor("b4_bwc", [128, 4096], BF16))
    stage_sb = st0.enter_context(nc.sbuf_tensor("b4_stage", [128, 4096], BF16))
    al_sb = st0.enter_context(nc.sbuf_tensor("b4_al", [128, 1], F32))

    def emit_binarize(src, nrow_chunks, width, dst_buf):
        nw = width // 128
        for ch in range(nrow_chunks):
            @block.gpsimd
            def _(g):
                pe.wait(g)
                so.wait(g)
                ld.dma(g.dma_start(out=wc_sb[:, 0:width],
                                   in_=src[ch * 128:(ch + 1) * 128, :]))
            ld_now = ld.n

            @block.vector
            def _(v):
                v.wait_ge(ld.h, ld_now)
                ve.inc(v.tensor_reduce(al_sb[:, 0:1], wc_sb[:, 0:width], AX.X,
                                       ALU.add, apply_absolute_value=True))

            @block.scalar
            def _(s):
                ve.wait(s)
                se.inc(s.activation(sgn_sb[:, 0:width], wc_sb[:, 0:width], AF.Sign))

            @block.vector
            def _(v):
                se.wait(v)
                ve.inc(v.tensor_scalar(bwc_sb[:, 0:width], sgn_sb[:, 0:width],
                                       al_sb[:, 0:1], 1.0 / width,
                                       ALU.mult, ALU.mult))
            ve_bw = ve.n
            ve_cpb = ve.n
            pe_tb = pe.n

            @block.tensor
            def _(te):
                te.wait_ge(ve.h, ve_bw)
                for c in range(nw):
                    par = pTt[c % 2]
                    if c >= 2:
                        te.wait_ge(ve.h, ve_cpb + c - 1)
                    inst = te.transpose(par[:, 0:128],
                                        bwc_sb[:, c * 128:(c + 1) * 128],
                                        eye_bf[:, :])
                    pe.inc(inst)

            @block.vector
            def _(v):
                for c in range(nw):
                    v.wait_ge(pe.h, pe_tb + c + 1)
                    ve.inc(v.tensor_copy(stage_sb[:, c * 128:(c + 1) * 128],
                                         pTt[c % 2][:, 0:128]))
            ve_stg = ve.n

            @block.sync
            def _(sy):
                sy.wait_ge(ve.h, ve_stg)
                so.dma(sy.dma_start(
                    out=dst_buf.rearrange("(cc p) hh -> p cc hh", p=128)
                        [:, :, ch * 128:(ch + 1) * 128],
                    in_=stage_sb[:, 0:width].rearrange("p (cc q) -> p cc q", q=128)))

    emit_binarize(w1, 32, 1024, bw1T_buf)
    emit_binarize(w2, 8, 4096, bw2T_buf)
    st0.close()

    x_tm = stB.enter_context(nc.sbuf_tensor("b4_xtm", [128, 4 * 1024], BF16))
    xT = stB.enter_context(nc.sbuf_tensor("b4_xT", [128, 8 * 512], BF16))
    xf_sb = stB.enter_context(nc.sbuf_tensor("b4_xf", [128, 4 * 1024], F32))
    res_sb = stB.enter_context(nc.sbuf_tensor("b4_res", [128, 1024], F32))
    wbuf = stB.enter_context(nc.sbuf_tensor("b4_wbuf", [128, 32 * 1024], BF16))
    wbuf2 = stB.enter_context(nc.sbuf_tensor("b4_wbuf2", [128, 32 * 1024], BF16))
    hg_sb = stB.enter_context(nc.sbuf_tensor("b4_hg", [128, 32 * 512], BF16))

    @block.sync
    def _(sy):
        so.wait(sy)
        for c in range(8):
            ls.dma(sy.dma_start(out=wbuf[:, c * 4096:(c + 1) * 4096],
                                in_=bw1T_buf[c * 128:(c + 1) * 128, :]))
        for hc in range(32):
            ls.dma(sy.dma_start(out=wbuf2[:, hc * 1024:(hc + 1) * 1024],
                                in_=bw2T_buf[hc * 128:(hc + 1) * 128, :]))
    ld_w1 = ls.n
    ld_w2 = ls.n

    tr_guard = 0
    for grp in range(NGRP):
        @block.gpsimd
        def _(g):
            g.wait_ge(pe.h, tr_guard)
            ve.wait(g)
            se.wait(g)
            so.wait(g)
            for t in range(4):
                ld.dma(g.dma_start(
                    out=x_tm[:, t * 1024:(t + 1) * 1024],
                    in_=xq[grp * 512 + t * 128: grp * 512 + (t + 1) * 128, :]))
                ld.dma(g.dma_start(
                    out=xf_sb[:, t * 1024:(t + 1) * 1024],
                    in_=xq[grp * 512 + t * 128: grp * 512 + (t + 1) * 128, :]))
        ld_x = ld.n
        ve_base = ve.n
        pe_base = pe.n

        @block.tensor
        def _(te):
            te.wait_ge(ld.h, ld_x)
            te.wait_ge(ve.h, ve_base)
            for c in range(8):
                par = pTt[c % 2]
                if c >= 2:
                    te.wait_ge(ve.h, ve_base + c - 1)
                for t in range(4):
                    inst = te.transpose(
                        par[:, t * 128:(t + 1) * 128],
                        x_tm[:, t * 1024 + c * 128: t * 1024 + (c + 1) * 128],
                        eye_bf[:, :])
                pe.inc(inst)

        @block.vector
        def _(v):
            for c in range(8):
                v.wait_ge(pe.h, pe_base + c + 1)
                ve.inc(v.tensor_copy(xT[:, c * 512:(c + 1) * 512], pTt[c % 2][:, :]))
        se_hb = se.n
        pe_hb = pe.n

        @block.tensor
        def _(te):
            te.wait_ge(ls.h, ld_w1)
            ve.wait(te)
            se.wait(te)
            for hc in range(32):
                if hc >= 4:
                    te.wait_ge(se.h, se_hb + hc - 3)
                for c in range(8):
                    inst = te.matmul(
                        pS[:, (hc % 4) * 512:(hc % 4 + 1) * 512],
                        wbuf[:, c * 4096 + hc * 128: c * 4096 + hc * 128 + 128],
                        xT[:, c * 512:(c + 1) * 512],
                        start=(c == 0), stop=(c == 7))
                pe.inc(inst)

        @block.scalar
        def _(s):
            for hc in range(32):
                s.wait_ge(pe.h, pe_hb + hc + 1)
                se.inc(s.activation(hg_sb[:, hc * 512:(hc + 1) * 512],
                                    pS[:, (hc % 4) * 512:(hc % 4 + 1) * 512],
                                    AF.Gelu, bias=b1s[:, hc:hc + 1]))

        se_hdone = se.n

        @block.vector
        def _(v):
            ve.wait(v)
            for t in range(4):
                ve.inc(v.tensor_tensor(xf_sb[:, t * 1024:(t + 1) * 1024],
                                       xf_sb[:, t * 1024:(t + 1) * 1024],
                                       b2b[:, :], ALU.add))

        evac_mark = {}
        for t in range(4):
            @block.tensor
            def _(te):
                te.wait_ge(ls.h, ld_w2)
                te.wait_ge(se.h, se_hdone)
                if t > 0:
                    te.wait_ge(ve.h, evac_mark[t - 1])
                for hc in range(32):
                    for h in range(2):
                        inst = te.matmul(
                            pPV[:, h * 512:(h + 1) * 512],
                            hg_sb[:, hc * 512 + t * 128: hc * 512 + t * 128 + 128],
                            wbuf2[:, hc * 1024 + h * 512: hc * 1024 + (h + 1) * 512],
                            start=(hc == 0), stop=(hc == 31))
                pe.inc(inst)
            pv_t = pe.n

            @block.vector
            def _(v):
                v.wait_ge(pe.h, pv_t)
                ve.inc(v.tensor_tensor(xf_sb[:, t * 1024:(t + 1) * 1024],
                                       xf_sb[:, t * 1024:(t + 1) * 1024],
                                       pPV[:, :], ALU.add))
            evac_mark[t] = ve.n

        for t in range(4):
            @block.vector
            def _(v):
                ve.wait(v)
                ve.inc(v.reduce_sum(ln_sb[:, t:t + 1],
                                    xf_sb[:, t * 1024:(t + 1) * 1024], axis=AX.X))
                ve.wait(v)
                ve.inc(v.tensor_scalar_mul(ln_sb[:, 4 + t:5 + t], ln_sb[:, t:t + 1],
                                           -1.0 / 1024.0))
                ve.wait(v)
                ve.inc(v.tensor_scalar_add(xf_sb[:, t * 1024:(t + 1) * 1024],
                                           xf_sb[:, t * 1024:(t + 1) * 1024],
                                           ln_sb[:, 4 + t:5 + t]))

            @block.scalar
            def _(s):
                ve.wait(s)
                se.inc(s.activation(x_tm[:, 0:1024],
                                    xf_sb[:, t * 1024:(t + 1) * 1024],
                                    AF.Square, accum_out=ln_sb[:, t:t + 1]))

            @block.vector
            def _(v):
                se.wait(v)
                ve.inc(v.tensor_scalar(ln_sb[:, 8 + t:9 + t], ln_sb[:, t:t + 1],
                                       1.0 / 1024.0, LN_EPS, ALU.mult, ALU.add))

            @block.scalar
            def _(s):
                ve.wait(s)
                se.inc(s.activation(ln_sb[:, 8 + t:9 + t], ln_sb[:, 8 + t:9 + t],
                                    AF.Sqrt))

            @block.vector
            def _(v):
                se.wait(v)
                ve.inc(v.reciprocal(ln_sb[:, 12 + t:13 + t], ln_sb[:, 8 + t:9 + t]))
                ve.wait(v)
                ve.inc(v.tensor_scalar_mul(xf_sb[:, t * 1024:(t + 1) * 1024],
                                           xf_sb[:, t * 1024:(t + 1) * 1024],
                                           ln_sb[:, 12 + t:13 + t]))

            @block.sync
            def _(sy):
                ve.wait(sy)
                so.dma(sy.dma_start(
                    out=x4q[grp * 512 + t * 128: grp * 512 + (t + 1) * 128, :],
                    in_=xf_sb[:, t * 1024:(t + 1) * 1024]))

        tr_guard = pe.n

    return stB



def build_fused(debug=False):
    nc = bass.Bass("TRN2", target_bir_lowering=False)
    io = {}

    def ein(name, shape, dt=F32):
        io[name] = nc.dram_tensor(name, shape, dt, kind="ExternalInput").ap()

    def internal(name, shape, dt):
        io[name] = nc.dram_tensor(name, shape, dt, kind="Internal").ap()

    ein("xb", [S, D]); ein("xq0", [TPC, D])
    ein("wqT", [D, D]); ein("wkT", [D, D]); ein("wvT", [D, D]); ein("woT", [D, D])
    ein("eye", [128, 128]); ein("dmask", [128, 512])
    ein("sgT", [D, 2048]); ein("suT", [D, 2048]); ein("sdT", [2048, D])
    ein("routerT", [D, 8]); ein("rbias", [1, 8])
    ein("wgT", [8, D, 2048]); ein("wuT", [8, D, 2048]); ein("wdT", [8, 2048, D])
    ein("onehot", [8, 8 * 128])
    ein("WqT", [D, D]); ein("WkT", [D, D]); ein("WvT", [D, D])
    ein("W1T", [D, D]); ein("W2T", [D, 4096]); ein("W3T", [4096, D])
    ein("bq_pc", [128, 8]); ein("bk_pc", [128, 8])
    ein("b1_pc", [128, 8]); ein("b2_pc", [128, 32])
    ein("bv_row", [1, D]); ein("b3_row", [1, D])
    ein("ecol", [8, 1])
    ein("f_w1", [4096, D]); ein("f_w2", [D, 4096])
    ein("f_b1_pc", [128, 32]); ein("f_b2_row", [1, D])
    io["x4q"] = nc.dram_tensor("x4q", [TPC, D], F32, kind="ExternalOutput").ap()
    internal("x1q_buf", [TPC, D], F32)
    internal("x2q_buf", [TPC, D], F32)
    internal("x3q_buf", [TPC, D], F32)
    internal("x2bf", [TPC, D], BF16)
    internal("xg", [NCORE * TPC, D], BF16)
    internal("xgb", [S, D], BF16)
    internal("kT_buf", [D, S], BF16)
    internal("v_buf", [S, D], BF16)
    internal("qT_buf", [D, TPC], BF16)
    internal("xe_all", [8 * CEXP, D], BF16)
    internal("ye_all", [8 * CEXP, D], BF16)
    internal("wbg", [9, D, 2048], BF16)
    internal("wbu", [9, D, 2048], BF16)
    internal("wbd", [9, 2048, D], BF16)
    internal("w1bf", [D, D], BF16)
    internal("w2bf", [D, 4096], BF16)
    internal("w3bf", [4096, D], BF16)
    internal("bw1T_buf", [D, 4096], BF16)
    internal("bw2T_buf", [4096, D], BF16)
    if debug:
        for k in ("dbg_x1", "dbg_x2", "dbg_x3"):
            io[k] = nc.dram_tensor(k, [TPC, D], F32, kind="ExternalOutput").ap()

    st = ExitStack()
    E = Env(nc, st)

    s1 = emit_l0_attn(E, io)
    emit_barrier(E)
    s1.close()
    s2 = emit_l0_ffn(E, io)
    emit_barrier(E)
    s2.close()
    s3 = emit_l1_attn(E, io)
    emit_barrier(E)
    s3.close()
    s4 = emit_l1_ffn(E, io)

    if debug:
        @E.block.gpsimd
        def _(g):
            for s in (E.ld, E.ls, E.pe, E.ve, E.se, E.so):
                s.wait(g)
            E.so.dma(g.dma_start(out=io["dbg_x1"], in_=io["x1q_buf"]))
            E.so.dma(g.dma_start(out=io["dbg_x2"], in_=io["x2q_buf"]))
            E.so.dma(g.dma_start(out=io["dbg_x3"], in_=io["x3q_buf"]))

    @E.block.sync
    def _(sy):
        E.so.wait(sy)

    s4.close()
    st.close()
    return nc



_EYE = np.eye(128, dtype=np.float32)


def _make_onehot():
    oh = np.zeros((8, 8 * 128), dtype=np.float32)
    for e in range(8):
        oh[e, e * 128:(e + 1) * 128] = 1.0
    return oh


def _pc(b):
    b = np.asarray(b)
    return np.ascontiguousarray(b.reshape(-1, 128).T)


def _t(a):
    return np.ascontiguousarray(np.asarray(a).T)


def stage_fused(inputs):
    x = np.asarray(inputs["L0_attn_in_features"])
    shared = {
        "wqT": _t(inputs["L0_attn_w_q"]), "wkT": _t(inputs["L0_attn_w_k"]),
        "wvT": _t(inputs["L0_attn_w_v"]), "woT": _t(inputs["L0_attn_w_o"]),
        "eye": _EYE,
        "sgT": _t(inputs["L0_ffn_sg_w"]), "suT": _t(inputs["L0_ffn_su_w"]),
        "sdT": _t(inputs["L0_ffn_sd_w"]),
        "routerT": _t(inputs["L0_ffn_router_w"]),
        "rbias": np.asarray(inputs["L0_ffn_routing_bias"]).reshape(1, 8),
        "wgT": np.ascontiguousarray(np.swapaxes(np.asarray(inputs["L0_ffn_wg"]), 1, 2)),
        "wuT": np.ascontiguousarray(np.swapaxes(np.asarray(inputs["L0_ffn_wu"]), 1, 2)),
        "wdT": np.ascontiguousarray(np.swapaxes(np.asarray(inputs["L0_ffn_wd"]), 1, 2)),
        "onehot": _make_onehot(),
        "WqT": _t(inputs["L1_attn_Wq"]), "WkT": _t(inputs["L1_attn_Wk"]),
        "WvT": _t(inputs["L1_attn_Wv"]),
        "W1T": _t(inputs["L1_attn_W1"]), "W2T": _t(inputs["L1_attn_W2"]),
        "W3T": _t(inputs["L1_attn_W3"]),
        "bq_pc": _pc(inputs["L1_attn_bq"]), "bk_pc": _pc(inputs["L1_attn_bk"]),
        "b1_pc": _pc(inputs["L1_attn_b1"]), "b2_pc": _pc(inputs["L1_attn_b2"]),
        "bv_row": np.asarray(inputs["L1_attn_bv"]).reshape(1, D).astype(np.float32),
        "b3_row": np.asarray(inputs["L1_attn_b3"]).reshape(1, D).astype(np.float32),
        "ecol": (np.arange(8, dtype=np.float32) * CEXP).reshape(8, 1),
        "f_w1": np.ascontiguousarray(np.asarray(inputs["L1_ffn_w1"])),
        "f_w2": np.ascontiguousarray(np.asarray(inputs["L1_ffn_w2"])),
        "f_b1_pc": _pc(inputs["L1_ffn_b1"]),
        "f_b2_row": np.asarray(inputs["L1_ffn_b2"]).reshape(1, D).astype(np.float32),
    }
    maps = []
    for c in range(NCORE):
        b, g = c // GPB, c % GPB
        rows = own_token_idx(g)
        m = dict(shared)
        m["xb"] = np.ascontiguousarray(x[b])
        m["xq0"] = np.ascontiguousarray(x[b][rows])
        m["dmask"] = make_dmask(g)
        maps.append(m)
    return maps


def kernel(**inputs):
    nc = build_fused()
    maps = stage_fused(inputs)
    res = run_bass_kernel_spmd(nc, maps, core_ids=list(range(NCORE)))
    out = np.empty((B, S, D), dtype=np.float32)
    for c in range(NCORE):
        b, g = c // GPB, c % GPB
        out[b, own_token_idx(g)] = res.results[c]["x4q"]
    return out
